# version 03 (final)
# baseline (speedup 1.0000x reference)
import contextlib
import math
import numpy as np
import ml_dtypes
import concourse.bass as bass
import concourse.mybir as mybir
from concourse.bass import IndirectOffsetOnAxis
from concourse.bass_utils import run_bass_kernel_spmd

F32 = mybir.dt.float32
BF16 = mybir.dt.bfloat16
I32 = mybir.dt.int32
U32 = mybir.dt.uint32
AF = mybir.ActivationFunctionType
ALU = mybir.AluOpType
AX = mybir.AxisListType

D = 1024
DH = 64
NH = 16
TPC = 2048
NT = TPC // 128
HT = 17
HS = HT * 128
SEQ = 8192
KTN = SEQ // 128
DFF = 4096
NEXP = 8
NST = 60
NSLOT = NEXP * TPC
LN_EPS = 1e-5
RMS_EPS = 1e-8
MOE_SCALE = 1.0 / math.sqrt(1.0 + 2.0 / 8.0)
QCS = [512, 512, 512, 512, 128]

PHASES = ("P0", "A0", "A1", "A2", "A3", "A4", "B", "C", "D0", "D1", "D2", "D3")
DEBUG = False

SEMNAMES = ["ld", "wl", "kv", "mm", "ev", "ex", "st", "qd",
            "fz", "g1", "g2", "g3", "g4", "g5", "ln", "xr", "vs"]


class Emit:

    def __init__(self, sems):
        self.sems = sems
        self.active = None
        self.eng = None
        self.cnt = {k: 0 for k in sems}

    def op(self, engine, fn, inc=None, n=1):
        if inc is not None:
            self.cnt[inc] += n
        if self.active == engine:
            r = fn(self.eng)
            if inc is not None and r is not None:
                r.then_inc(self.sems[inc], n)

    def dma(self, engine, fn, inc):
        self.op(engine, fn, inc, 16)

    def vop(self, engine, fn):
        self.op(engine, fn, inc="vs")
        self.wait(engine, "vs")

    def wait(self, engine, sem, val=None):
        v = self.cnt[sem] if val is None else val
        if self.active == engine:
            self.eng.wait_ge(self.sems[sem], v)
        return v

    def barrier(self):
        for engine in ("sync", "gpsimd", "vector", "scalar", "tensor"):
            for s in SEMNAMES:
                self.wait(engine, s)


class Ctx:
    pass



def build_program():
    nc = bass.Bass("TRN2", target_bir_lowering=False)
    stack = contextlib.ExitStack()
    c = Ctx()
    c.nc = nc

    d = {}

    def din(n, s, dt=F32):
        d[n] = nc.dram_tensor(n, s, dt, kind="ExternalInput").ap()

    def dscr(n, s, dt=F32):
        d[n] = nc.dram_tensor(n, s, dt).ap()

    din("x_halo", [HS, D]); din("x_full", [SEQ, D])
    din("wq_t", [D, D]); din("wk_t", [D, D]); din("wv_t", [D, D]); din("wo_t", [D, D])
    din("g_cols", [128, 8]); din("b_cols", [128, 8])
    din("qg_row", [1, DH]); din("kg_row", [1, DH]); din("bo_row", [1, D])
    din("w1_t", [D, DFF]); din("w2_t", [DFF, D])
    din("convw", [3, DFF]); din("convb_r", [1, DFF]); din("fl_lr", [1, 2])
    din("l1wq", [D, D]); din("l1wk", [D, D]); din("l1wv", [D, D]); din("l1wo", [D, D])
    din("l1b4", [4, D])
    din("wr_t", [D, NEXP])
    din("w1e", [NEXP * 128, 8 * DFF + 32]); din("w2e", [NEXP * 128, 32 * D + 8])
    din("b1e", [NEXP, DFF]); din("b2e", [NEXP, D])
    dscr("rscr", [1, DFF])
    din("ws1_t", [D, DFF]); din("ws2_t", [DFF, D])
    din("bs1_row", [1, DFF]); din("bs2_row", [1, D])
    din("ident", [128, 128]); din("lstrict", [128, 128]); din("tri8x", [8, 8])
    din("onesrow", [1, 128]); din("tokid", [128, NT]); din("hsum2", [128, 2])
    din("idx8", [128, NEXP])
    d["y_out"] = nc.dram_tensor("y_out", [TPC, D], F32, kind="ExternalOutput").ap()
    if DEBUG:
        for n, sh in [("dbg_stat", [128, 256]), ("dbg_cons", [128, 520]),
                      ("dbg_k0", [8, 128, 128]), ("dbg_v0", [128, NH * 65]),
                      ("dbg_rk0", [128, NH]), ("dbg_qn", [128, 8192]),
                      ("dbg_oscr", [256, D]), ("dbg_x1", [256, D]),
                      ("dbg_h", [256, DFF]), ("dbg_xq2", [256, DFF]),
                      ("dbg_w1q", [128, 8192]), ("dbg_amax", [32, 128]),
                      ("dbg_xq", [128, D]), ("dbg_dq", [128, 4]), ("dbg_xn", [128, D]),
                      ("dbg_rt", [128, 160]), ("dbg_row", [1, 128]),
                      ("dbg_y", [NSLOT + 128, D]), ("dbg_sa", [TPC, D]),
                      ("dbg_x3", [TPC, D]), ("dbg_xs", [NSLOT + 128, D])]:
            d[n] = nc.dram_tensor(n, sh, F32, kind="ExternalOutput").ap()

    dscr("kscr", [KTN, 8, 128, 128])
    dscr("vscr", [KTN, 128, NH * 65])
    dscr("rkscr", [KTN, 128, NH])
    dscr("oscr", [HS, D])
    dscr("x1scr", [HS, D])
    dscr("hfscr", [HS + 128, DFF])
    dscr("xq2scr", [HS, DFF])
    dscr("x2scr", [TPC, D]); dscr("x3scr", [TPC, D]); dscr("xn3scr", [TPC, D])
    dscr("xsscr", [NSLOT + 128, D])
    dscr("hmscr", [NSLOT, DFF])
    dscr("yscr", [NSLOT + 128, D])
    dscr("sascr", [TPC, D])
    c.d = d

    AR = 51520
    c.arena = stack.enter_context(nc.sbuf_tensor("arena", [128, AR], F32))
    c.cons = stack.enter_context(nc.sbuf_tensor("cons", [128, 520], F32))
    c.statc = stack.enter_context(nc.sbuf_tensor("statc", [128, 256], F32))
    c.PS = [stack.enter_context(nc.psum_tensor(f"ps{i}", [128, 512], F32))
            for i in range(8)]
    c.sems = {n: stack.enter_context(nc.semaphore(n)) for n in SEMNAMES}

    c.o_fm = 0
    c.o_big = 17408
    c.o_oacc = 34048
    c.o_xt = 38208
    c.o_et = 40256
    c.o_pt = 42304
    c.o_k2 = 44352
    c.o_sq = 48448
    c.o_rw = 49472

    c.e = Emit(c.sems)

    def program():
        _views(c)
        if "P0" in PHASES:
            ph_P0(c)
        c.e.barrier()
        if "A0" in PHASES:
            ph_A0(c)
        c.e.barrier()
        if "A1" in PHASES:
            ph_A1(c)
        c.e.barrier()
        if "A2" in PHASES:
            ph_A2(c)
        c.e.barrier()
        if "A3" in PHASES:
            ph_A3(c)
        c.e.barrier()
        if "A4" in PHASES:
            ph_A4(c)
        c.e.barrier()
        if "B" in PHASES or "B0" in PHASES or "BA" in PHASES or "BB" in PHASES:
            ph_B(c)
            c.o_sq = 48448
        c.e.barrier()
        if "C" in PHASES:
            ph_C(c)
        c.e.barrier()
        if "D0" in PHASES:
            ph_D0(c)
        c.e.barrier()
        if "D1" in PHASES:
            ph_D1(c)
        c.e.barrier()
        if "D2" in PHASES:
            ph_D2(c)
        c.e.barrier()
        if "D3" in PHASES:
            ph_D3(c)
        c.e.barrier()
        if "TAIL_X1" in PHASES:
            ph_tail_x1(c)
        if "TAIL_X2" in PHASES:
            ph_tail_x2(c)
        if "TAIL_X3" in PHASES:
            ph_tail_x3(c)
        c.e.wait("gpsimd", "st")
        c.e.wait("sync", "st")

    with nc.Block() as block:
        def run_for(name):
            def _f(eng):
                c.e.active = name
                c.e.eng = eng
                c.e.cnt = {k: 0 for k in c.sems}
                program()
            return _f
        block.sync(run_for("sync"))
        block.gpsimd(run_for("gpsimd"))
        block.vector(run_for("vector"))
        block.scalar(run_for("scalar"))
        block.tensor(run_for("tensor"))

    return nc, stack


def A(c, off, n):
    return c.arena[:, off:off + n]


def A3d(c, off, a, b):
    return c.arena[:, off:off + a * b].rearrange("p (a b) -> p a b", a=a)


def _views(c):
    c.ln_count = 0
    cons, statc = c.cons, c.statc
    c.ident_s = cons[:, 0:128]
    c.lstrict_s = cons[:, 128:256]
    c.hsum2_s = cons[:, 256:258]
    c.gcol_s = cons[:, 258:266]
    c.bcol_s = cons[:, 266:274]
    c.ggk64_s = cons[:, 288:352]
    c.negb_s = cons[:, 352:353]
    c.flags_s = cons[0:1, 353:355]
    c.tokid_s = cons[:, 355:371]
    c.tri8_s = cons[:, 371:379]
    c.onesr_s = cons[0:1, 384:512]

    c.kb_cols = statc[:, 0:8]
    c.maxq_c = statc[:, 8:9]
    c.maxk_c = statc[:, 9:10]
    c.s_sum = statc[:, 10:11]
    c.s_sq = statc[:, 11:12]
    c.s_a = statc[:, 12:13]
    c.s_na = statc[:, 13:14]
    c.s_t0 = statc[:, 14:15]
    c.s_t1 = statc[:, 15:16]
    c.rnq_s = statc[:, 16:32]
    c.rkt_s = statc[:, 32:48]
    c.eps_c = statc[:, 176:177]
    c.qgr_s = statc[0:1, 48:112]
    c.kgr_s = statc[0:1, 112:176]

    c.identb = statc[:, 184:248].bitcast(BF16)
    c.eighth_c = statc[:, 248:249]
    c.onec = statc[:, 249:250]
    c.idx8_s = cons[:, 512:520]
    c.row0 = c.arena[0:1, c.o_rw:c.o_rw + D]
    c.row1 = c.arena[0:1, c.o_rw + D:c.o_rw + 2 * D]



def ph_P0(c):
    e, d = c.e, c.d
    loads = [
        (c.ident_s, "ident"), (c.lstrict_s, "lstrict"), (c.hsum2_s, "hsum2"),
        (c.gcol_s, "g_cols"), (c.bcol_s, "b_cols"),
        (c.qgr_s, "qg_row"), (c.kgr_s, "kg_row"),
        (c.flags_s, "fl_lr"), (c.tokid_s, "tokid"),
        (c.tri8_s[0:8, :], "tri8x"), (c.onesr_s, "onesrow"),
        (c.idx8_s, "idx8"),
    ]
    for dst, name in loads:
        e.dma("sync", lambda s, dst=dst, name=name: s.dma_start(out=dst, in_=d[name]), "ld")
    e.wait("vector", "ld")
    e.op("vector", lambda v: v.memset(c.eps_c, LN_EPS))
    e.op("vector", lambda v: v.tensor_tensor(c.qgr_s, c.qgr_s, c.kgr_s, ALU.mult))
    e.op("vector", lambda v: v.tensor_scalar(c.qgr_s, c.qgr_s, 0.125, None, ALU.mult), "g1")
    e.wait("tensor", "g1")
    e.op("tensor", lambda t: t.matmul(c.PS[0][:, 0:64], c.onesr_s, c.qgr_s,
                                      start=True, stop=True), "mm")
    e.wait("vector", "mm")
    e.op("vector", lambda v: v.tensor_copy(c.ggk64_s, c.PS[0][:, 0:64]))
    e.op("vector", lambda v: v.tensor_copy(c.identb, c.ident_s))
    e.op("vector", lambda v: v.memset(c.maxq_c, 0.0))
    e.op("vector", lambda v: v.memset(c.eighth_c, 0.125))
    e.op("vector", lambda v: v.memset(c.onec, 1.0))
    e.op("vector", lambda v: v.memset(c.maxk_c, 0.0), "g1")



def _fold_weight(c, name_w, o_dst, fm_bias_cols, row_bias):
    e, d = c.e, c.d
    stg = [A(c, c.o_pt, D), A(c, c.o_pt + D, D)]
    g2b = e.cnt["g2"]
    mmb = e.cnt["mm"]
    per = (2 if row_bias is not None else 0) + (8 if fm_bias_cols is not None else 0)
    for ch in range(8):
        if ch >= 2:
            e.wait("sync", "g2", g2b + ch - 1)
            e.wait("sync", "mm", mmb + per * (ch - 1))
        e.dma("sync", lambda s, ch=ch: s.dma_start(
            out=stg[ch % 2], in_=d[name_w][ch * 128:(ch + 1) * 128, :]), "wl")
        e.wait("tensor", "wl")
        if row_bias is not None:
            e.op("tensor", lambda t, ch=ch: t.matmul(
                c.PS[1][0:1, 0:512], c.bcol_s[:, ch:ch + 1], stg[ch % 2][:, 0:512],
                start=(ch == 0), stop=(ch == 7)), "mm")
            e.op("tensor", lambda t, ch=ch: t.matmul(
                c.PS[2][0:1, 0:512], c.bcol_s[:, ch:ch + 1], stg[ch % 2][:, 512:1024],
                start=(ch == 0), stop=(ch == 7)), "mm")
        if fm_bias_cols is not None:
            for m in range(8):
                e.op("tensor", lambda t, ch=ch, m=m: t.matmul(
                    c.PS[3][:, m:m + 1], stg[ch % 2][:, m * 128:(m + 1) * 128],
                    c.bcol_s[:, ch:ch + 1], start=(ch == 0), stop=(ch == 7)), "mm")
        e.wait("vector", "wl")
        e.op("vector", lambda v, ch=ch: v.tensor_scalar(
            A(c, o_dst + ch * D, D), stg[ch % 2],
            c.gcol_s[:, ch:ch + 1], None, ALU.mult), "g2")
    e.wait("vector", "mm")
    if row_bias is not None:
        e.op("vector", lambda v: v.tensor_copy(row_bias[0:1, 0:512], c.PS[1][0:1, 0:512]))
        e.op("vector", lambda v: v.tensor_copy(row_bias[0:1, 512:1024], c.PS[2][0:1, 0:512]), "g3")
    if fm_bias_cols is not None:
        e.op("vector", lambda v: v.tensor_copy(fm_bias_cols, c.PS[3][:, 0:8]), "g3")


def ph_A0(c):
    c.o_wk = c.o_big
    c.o_wv = c.o_big + 8 * D
    _fold_weight(c, "wk_t", c.o_wk, c.kb_cols, None)
    c.e.barrier()
    _fold_weight(c, "wv_t", c.o_wv, None, c.row0)



def ln_tile(c, x_ap, dst_ap, double_ln):
    e = c.e
    n = c.ln_count
    c.ln_count += 1
    e.vop("vector", lambda v: v.tensor_reduce(c.s_sum, x_ap, AX.X, ALU.add))
    if n > 0:
        e.wait("scalar", "ln", n)
    e.op("scalar", lambda s: s.activation(
        A(c, c.o_sq, D), x_ap, AF.Square), inc="ex")
    e.vop("vector", lambda v: v.tensor_scalar(c.s_sum, c.s_sum, 1.0 / D, None, ALU.mult))
    e.wait("vector", "ex")
    e.op("vector", lambda v: v.tensor_reduce(c.s_sq, A(c, c.o_sq, D), AX.X, ALU.add), inc="ln")
    e.wait("vector", "ln")
    e.vop("vector", lambda v: v.tensor_tensor(c.s_t0, c.s_sum, c.s_sum, ALU.mult))
    e.vop("vector", lambda v: v.tensor_scalar(c.s_sq, c.s_sq, 1.0 / D, None, ALU.mult))
    e.op("vector", lambda v: v.tensor_tensor(c.s_t0, c.s_sq, c.s_t0, ALU.subtract), inc="g1")
    e.wait("vector", "g1")
    e.wait("scalar", "g1")
    e.op("scalar", lambda s: s.activation(c.s_t1, c.s_t0, AF.Sqrt, bias=c.eps_c, scale=1.0), inc="ex")
    e.wait("vector", "ex")
    e.vop("vector", lambda v: v.reciprocal(c.s_a, c.s_t1))
    if double_ln:
        e.vop("vector", lambda v: v.tensor_tensor(c.s_t1, c.s_a, c.s_a, ALU.mult))
        e.op("vector", lambda v: v.tensor_tensor(c.s_t1, c.s_t0, c.s_t1, ALU.mult), inc="g1")
        e.wait("vector", "g1")
        e.wait("scalar", "g1")
        e.op("scalar", lambda s: s.activation(c.s_t1, c.s_t1, AF.Sqrt, bias=c.eps_c, scale=1.0), inc="ex")
        e.wait("vector", "ex")
        e.vop("vector", lambda v: v.reciprocal(c.s_t1, c.s_t1))
        e.vop("vector", lambda v: v.tensor_tensor(c.s_a, c.s_a, c.s_t1, ALU.mult))
    e.vop("vector", lambda v: v.tensor_tensor(c.s_na, c.s_sum, c.s_a, ALU.mult))
    e.vop("vector", lambda v: v.tensor_scalar(c.s_na, c.s_na, -1.0, None, ALU.mult))
    e.op("vector", lambda v: v.tensor_scalar(
        dst_ap, x_ap, c.s_a, c.s_na, ALU.mult, ALU.add), inc="xr")


def transpose_tile(c, src_ap, dst_ap, pb0, pb1, which, bf=False):
    e = c.e
    pb = pb0 if which % 2 == 0 else pb1
    ident = c.identb if bf else c.ident_s
    out = pb[:, 0:64].bitcast(BF16) if bf else pb[:, 0:128]
    if which >= 2:
        e.wait("tensor", "ev", e.cnt["ev"] - 1)
    e.op("tensor", lambda t: t.transpose(out, src_ap, ident), "mm")
    e.wait("vector", "mm")
    e.op("vector", lambda v: v.tensor_copy(dst_ap, out), "ev")



def ph_A1(c):
    e, d = c.e, c.d
    xst = [A(c, c.o_xt, D), A(c, c.o_xt + D, D)]
    xnfm = A3d(c, c.o_fm, 8, 512)
    kfm = A3d(c, c.o_fm + 8 * 512, 8, 512)
    k2v = A3d(c, c.o_k2, 8, 512)
    vstage = A3d(c, c.o_oacc, 4, 1040)
    rkstg = A(c, c.o_et, 64)

    e.wait("vector", "g3")
    for sub in range(4):
        e.op("vector", lambda v, sub=sub: v.memset(
            vstage[:, sub, :].rearrange("p (h z) -> p h z", h=NH)[:, :, 64:65], 1.0), "g3")

    ldb = e.cnt["ld"]
    g4b = e.cnt["g4"]
    for g in range(16):
        for j in range(4):
            i = 4 * g + j
            if i >= 2:
                e.wait("sync", "g4", g4b + i - 1)
            e.dma("sync", lambda s, i=i: s.dma_start(
                out=xst[i % 2], in_=d["x_full"][i * 128:(i + 1) * 128, :]), "ld")
            lv = ldb + 16 * (i + 1)
            e.wait("vector", "ld", lv)
            e.wait("scalar", "ld", lv)
            ln_tile(c, xst[i % 2], xst[i % 2], double_ln=True)
            e.wait("tensor", "xr")
            for ch in range(8):
                transpose_tile(c, xst[i % 2][:, ch * 128:(ch + 1) * 128],
                               xnfm[:, ch, j * 128:(j + 1) * 128],
                               c.PS[0], c.PS[1], 8 * i + ch)
            e.op("tensor", lambda t: t.nop(), "g4")
        e.wait("tensor", "ev")
        evb = e.cnt["ev"]
        for m in range(8):
            pk = c.PS[2] if m % 2 == 0 else c.PS[3]
            if m >= 2:
                e.wait("tensor", "ev", evb + m - 1)
            for kc in range(8):
                e.op("tensor", lambda t, m=m, kc=kc, pk=pk: t.matmul(
                    pk[:, 0:512], A(c, c.o_wk + kc * D + m * 128, 128),
                    xnfm[:, kc, :], start=(kc == 0), stop=(kc == 7)), "mm")
            e.wait("vector", "mm")
            e.op("vector", lambda v, m=m, pk=pk: v.tensor_scalar(
                kfm[:, m, :], pk[:, 0:512], c.kb_cols[:, m:m + 1], None, ALU.add), "ev")
        evb = e.cnt["ev"]
        for j in range(4):
            for nh in range(2):
                w = 2 * j + nh
                pv = c.PS[4] if w % 2 == 0 else c.PS[5]
                if w >= 2:
                    e.wait("tensor", "ev", evb + w - 1)
                for kc in range(8):
                    e.op("tensor", lambda t, j=j, nh=nh, kc=kc, pv=pv: t.matmul(
                        pv[:, 0:512], xnfm[:, kc, j * 128:(j + 1) * 128],
                        A(c, c.o_wv + kc * D + nh * 512, 512),
                        start=(kc == 0), stop=False), "mm")
                e.op("tensor", lambda t, nh=nh, pv=pv: t.matmul(
                    pv[:, 0:512], c.onesr_s, c.row0[0:1, nh * 512:(nh + 1) * 512],
                    start=False, stop=True), "mm")
                e.wait("vector", "mm")
                e.op("vector", lambda v, j=j, nh=nh, pv=pv: v.tensor_copy(
                    vstage[:, j, nh * 520:(nh + 1) * 520].rearrange(
                        "p (h z) -> p h z", h=8)[:, :, 0:64],
                    pv[:, 0:512]), "ev")
        e.wait("scalar", "ev")
        e.op("scalar", lambda s: s.activation(
            A(c, c.o_k2, 8 * 512), A(c, c.o_fm + 8 * 512, 8 * 512), AF.Square), inc="ex")
        e.wait("tensor", "ex")
        g5b = e.cnt["g5"]
        stb = e.cnt["st"]
        for j in range(4):
            pr = c.PS[6] if j % 2 == 0 else c.PS[7]
            if j >= 2:
                e.wait("tensor", "g5", g5b + j - 1)
                e.wait("tensor", "st", stb + 16 * (j - 1))
            for ch in range(8):
                e.op("tensor", lambda t, j=j, ch=ch, pr=pr: t.matmul(
                    pr[:, 2 * ch:2 * ch + 2], k2v[:, ch, j * 128:(j + 1) * 128],
                    c.hsum2_s, start=(ch == 0), stop=(ch == 7)), "mm")
            e.wait("scalar", "mm")
            e.op("scalar", lambda s, pr=pr: s.activation(
                c.rkt_s, pr[:, 0:16], AF.Sqrt), inc="ex")
            e.wait("vector", "ex")
            e.vop("vector", lambda v: v.tensor_scalar(
                rkstg[:, 0:16], c.rkt_s, 0.125, None, ALU.mult))
            e.vop("vector", lambda v: v.tensor_scalar(
                rkstg[:, 0:16], rkstg[:, 0:16], RMS_EPS, None, ALU.max))
            e.vop("vector", lambda v: v.reciprocal(rkstg[:, 16:32], rkstg[:, 0:16]))
            e.vop("vector", lambda v: v.tensor_tensor(
                rkstg[:, 32:48], c.rkt_s, rkstg[:, 16:32], ALU.mult))
            e.vop("vector", lambda v: v.tensor_reduce(
                rkstg[:, 48:49], rkstg[:, 32:48], AX.X, ALU.max))
            e.op("vector", lambda v: v.tensor_tensor(
                c.maxk_c, c.maxk_c, rkstg[:, 48:49], ALU.max), inc="g5")
            e.wait("vector", "g5")
            e.wait("gpsimd", "g5")
            e.dma("gpsimd", lambda gp, g=g, j=j: gp.dma_start(
                out=d["rkscr"][4 * g + j], in_=rkstg[:, 16:32]), "st")
        e.wait("gpsimd", "ev")
        for j in range(4):
            e.dma("gpsimd", lambda gp, g=g, j=j: gp.dma_start(
                out=d["kscr"][4 * g + j], in_=kfm[:, :, j * 128:(j + 1) * 128]), "st")
            e.dma("gpsimd", lambda gp, g=g, j=j: gp.dma_start(
                out=d["vscr"][4 * g + j], in_=vstage[:, j, :]), "st")
        e.wait("tensor", "st")
        e.wait("vector", "st")



def ph_A2(c):
    e, d = c.e, c.d
    c.o_wq = c.o_big
    _fold_weight(c, "wq_t", c.o_wq, None, c.row0)
    e.barrier()

    xst = [A(c, c.o_xt, D), A(c, c.o_xt + D, D)]
    qtm = [A(c, c.o_et, D), A(c, c.o_et + D, D)]
    qnfm = A3d(c, c.o_fm, 8, HS)
    xnstg = A3d(c, c.o_k2, 8, 128)
    q2 = A3d(c, c.o_k2 + 1024, 8, 128)

    ldb = e.cnt["ld"]
    g4b = e.cnt["g4"]
    for i in range(HT):
        if i >= 2:
            e.wait("sync", "g4", g4b + i - 1)
        e.dma("sync", lambda s, i=i: s.dma_start(
            out=xst[i % 2], in_=d["x_halo"][i * 128:(i + 1) * 128, :]), "ld")
        lv = ldb + 16 * (i + 1)
        e.wait("vector", "ld", lv)
        e.wait("scalar", "ld", lv)
        ln_tile(c, xst[i % 2], xst[i % 2], double_ln=True)
        e.wait("tensor", "xr")
        for ch in range(8):
            transpose_tile(c, xst[i % 2][:, ch * 128:(ch + 1) * 128],
                           xnstg[:, ch, :], c.PS[0], c.PS[1], 8 * i + ch)
        e.op("tensor", lambda t: t.nop(), "g4")
        e.wait("tensor", "ev")
        for nh in range(2):
            pq = c.PS[2] if nh == 0 else c.PS[3]
            for kc in range(8):
                e.op("tensor", lambda t, nh=nh, kc=kc, pq=pq: t.matmul(
                    pq[:, 0:512], xnstg[:, kc, :],
                    A(c, c.o_wq + kc * D + nh * 512, 512),
                    start=(kc == 0), stop=False), "mm")
            e.op("tensor", lambda t, nh=nh, pq=pq: t.matmul(
                pq[:, 0:512], c.onesr_s, c.row0[0:1, nh * 512:(nh + 1) * 512],
                start=False, stop=True), "mm")
            e.wait("vector", "mm")
            e.op("vector", lambda v, nh=nh, pq=pq, i=i: v.tensor_copy(
                qtm[i % 2][:, nh * 512:(nh + 1) * 512], pq[:, 0:512]), "ev")
        e.wait("scalar", "ev")
        e.op("scalar", lambda s, i=i: s.activation(
            A(c, c.o_k2 + 1024, D), qtm[i % 2], AF.Square), inc="ex")
        e.wait("vector", "ex")
        for h in range(NH):
            if h < NH - 1:
                e.op("vector", lambda v, h=h: v.tensor_reduce(
                    c.rnq_s[:, h:h + 1], A(c, c.o_k2 + 1024 + h * 64, 64),
                    AX.X, ALU.add))
            else:
                e.vop("vector", lambda v, h=h: v.tensor_reduce(
                    c.rnq_s[:, h:h + 1], A(c, c.o_k2 + 1024 + h * 64, 64),
                    AX.X, ALU.add))
        e.op("vector", lambda v: v.tensor_copy(c.s_t0, c.s_t0), "g1")
        e.wait("scalar", "g1")
        e.op("scalar", lambda s: s.activation(c.rkt_s, c.rnq_s, AF.Sqrt), inc="ex")
        e.wait("vector", "ex")
        e.vop("vector", lambda v: v.tensor_scalar(c.rkt_s, c.rkt_s, 0.125, None, ALU.mult))
        e.vop("vector", lambda v: v.tensor_scalar(c.rkt_s, c.rkt_s, RMS_EPS, None, ALU.max))
        e.vop("vector", lambda v: v.reciprocal(c.rnq_s, c.rkt_s))
        for h in range(NH):
            e.op("vector", lambda v, h=h, i=i: v.scalar_tensor_tensor(
                qtm[i % 2][:, h * 64:(h + 1) * 64], qtm[i % 2][:, h * 64:(h + 1) * 64],
                c.rnq_s[:, h:h + 1], c.ggk64_s, ALU.mult, ALU.mult))
        e.op("vector", lambda v: v.tensor_copy(c.s_t0, c.s_t0), "g2")
        e.wait("scalar", "g2")
        e.op("scalar", lambda s, i=i: s.activation(
            A(c, c.o_k2 + 1024, D), qtm[i % 2], AF.Square), inc="ex")
        e.wait("vector", "ex")
        for h in range(NH):
            if h < NH - 1:
                e.op("vector", lambda v, h=h: v.tensor_reduce(
                    c.rkt_s[:, h:h + 1], A(c, c.o_k2 + 1024 + h * 64, 64),
                    AX.X, ALU.add))
            else:
                e.vop("vector", lambda v, h=h: v.tensor_reduce(
                    c.rkt_s[:, h:h + 1], A(c, c.o_k2 + 1024 + h * 64, 64),
                    AX.X, ALU.add))
        e.op("vector", lambda v: v.tensor_copy(c.s_t0, c.s_t0), "g1")
        e.wait("scalar", "g1")
        e.op("scalar", lambda s: s.activation(c.rkt_s, c.rkt_s, AF.Sqrt), inc="ex")
        e.wait("vector", "ex")
        e.vop("vector", lambda v: v.tensor_reduce(c.s_t0, c.rkt_s, AX.X, ALU.max))
        e.vop("vector", lambda v: v.tensor_tensor(c.maxq_c, c.maxq_c, c.s_t0, ALU.max))
        e.wait("tensor", "g2")
        for ch in range(8):
            transpose_tile(c, qtm[i % 2][:, ch * 128:(ch + 1) * 128],
                           qnfm[:, ch, i * 128:(i + 1) * 128],
                           c.PS[4], c.PS[5], ch)

    e.barrier()
    e.wait("gpsimd", "vs")
    e.dma("gpsimd", lambda gp: gp.dma_start(out=d["x2scr"][0:1, 0:128],
                                            in_=c.maxq_c), "st")
    e.dma("gpsimd", lambda gp: gp.dma_start(out=d["x2scr"][1:2, 0:128],
                                            in_=c.maxk_c), "st")
    e.wait("sync", "st")
    e.dma("sync", lambda s: s.dma_start(out=c.statc[0:1, 48:176],
                                        in_=d["x2scr"][0:1, 0:128]), "ld")
    e.wait("vector", "ld")
    e.vop("vector", lambda v: v.tensor_reduce(c.s_t0[0:1, :], c.statc[0:1, 48:176], AX.X, ALU.max))
    e.op("vector", lambda v: v.tensor_copy(c.s_t1[0:1, :], c.s_t1[0:1, :]), "g1")
    e.wait("sync", "g1")
    e.dma("sync", lambda s: s.dma_start(out=c.statc[0:1, 48:176],
                                        in_=d["x2scr"][1:2, 0:128]), "ld")
    e.wait("vector", "ld")
    e.vop("vector", lambda v: v.tensor_reduce(c.s_t1[0:1, :], c.statc[0:1, 48:176], AX.X, ALU.max))
    e.vop("vector", lambda v: v.tensor_tensor(c.s_t0[0:1, :], c.s_t0[0:1, :], c.s_t1[0:1, :], ALU.mult))
    e.op("vector", lambda v: v.tensor_scalar(c.s_t0[0:1, :], c.s_t0[0:1, :], -1.0, None, ALU.mult), "g1")
    e.wait("vector", "g1")
    e.wait("tensor", "g1")
    e.op("tensor", lambda t: t.matmul(c.PS[6][:, 0:1], c.onesr_s, c.s_t0[0:1, 0:1],
                                      start=True, stop=True), "mm")
    e.wait("vector", "mm")
    e.op("vector", lambda v: v.tensor_copy(c.negb_s, c.PS[6][:, 0:1]), "g1")



def ph_A3(c):
    e, d = c.e, c.d
    qnfm = A3d(c, c.o_fm, 8, HS)
    o_skf = c.o_big
    o_svp = c.o_big + 8 * 1024
    o_srk = o_svp + 8 * 1040
    skf = A3d(c, o_skf, 8, 1024)
    svp = A3d(c, o_svp, 8, 1040)
    srk = A3d(c, o_srk, 8, 16)
    oacc = A3d(c, c.o_oacc, 4, 1040)
    pt = A3d(c, c.o_pt, 4, 512)

    kvb = e.cnt["kv"]
    fzb = e.cnt["fz"]
    n_ld = 0
    qoff = 0
    for qc, qn in enumerate(QCS):
        nsub = qn // 128
        for ks in range(8):
            isup = qc * 8 + ks
            if isup > 0:
                e.wait("sync", "fz", fzb + isup)
            for kt in range(8):
                gk = ks * 8 + kt
                e.dma("sync", lambda s, gk=gk, kt=kt: s.dma_start(
                    out=skf[:, :, kt * 128:(kt + 1) * 128], in_=d["kscr"][gk]), "kv")
                e.dma("sync", lambda s, gk=gk, kt=kt: s.dma_start(
                    out=svp[:, kt, :], in_=d["vscr"][gk]), "kv")
                e.dma("sync", lambda s, gk=gk, kt=kt: s.dma_start(
                    out=srk[:, kt, :], in_=d["rkscr"][gk]), "kv")
            n_ld += 24
            kvv = kvb + 16 * n_ld
            e.wait("tensor", "kv", kvv)
            e.wait("scalar", "kv", kvv)
            evb = e.cnt["ev"]
            for hp in range(8):
                qdb = e.cnt["qd"]
                exb = e.cnt["ex"]
                g2b = e.cnt["g2"]
                for kt in range(8):
                    par = kt % 2
                    psa, psb = c.PS[0 + par], c.PS[2 + par]
                    if kt >= 2:
                        e.wait("tensor", "ex", exb + 2 * (kt - 1))
                    e.op("tensor", lambda t, hp=hp, kt=kt, psa=psa: t.matmul(
                        psa[:, 0:qn], skf[0:64, hp, kt * 128:(kt + 1) * 128],
                        qnfm[0:64, hp, qoff:qoff + qn], start=True, stop=True), "g2")
                    e.op("tensor", lambda t, hp=hp, kt=kt, psb=psb: t.matmul(
                        psb[:, 0:qn], skf[64:128, hp, kt * 128:(kt + 1) * 128],
                        qnfm[64:128, hp, qoff:qoff + qn], start=True, stop=True,
                        tile_position=(64, 0)), "g2")
                    e.wait("scalar", "g2", g2b + 2 * (kt + 1))
                    if kt >= 2:
                        e.wait("scalar", "qd", qdb + 2 * nsub * (kt - 1))
                    e.op("scalar", lambda s, hp=hp, kt=kt, par=par, psa=psa: s.activation(
                        pt[:, 0 + par, 0:qn], psa[:, 0:qn], AF.Exp,
                        bias=c.negb_s, scale=srk[:, kt, 2 * hp:2 * hp + 1]), inc="ex")
                    e.op("scalar", lambda s, hp=hp, kt=kt, par=par, psb=psb: s.activation(
                        pt[:, 2 + par, 0:qn], psb[:, 0:qn], AF.Exp,
                        bias=c.negb_s, scale=srk[:, kt, 2 * hp + 1:2 * hp + 2]), inc="ex")
                    e.wait("tensor", "ex")
                    poa = c.PS[4 + 2 * (hp % 2)]
                    pob = c.PS[5 + 2 * (hp % 2)]
                    if kt == 0:
                        if hp >= 2:
                            e.wait("tensor", "ev", evb + hp - 1)
                        elif isup > 0:
                            e.wait("tensor", "ev", evb - 1 + hp)
                    for sub in range(nsub):
                        e.op("tensor", lambda t, hp=hp, kt=kt, sub=sub, par=par, poa=poa: t.matmul(
                            poa[:, sub * 65:sub * 65 + 65],
                            pt[:, 0 + par, sub * 128:(sub + 1) * 128],
                            svp[:, kt, 65 * (2 * hp):65 * (2 * hp) + 65],
                            start=(kt == 0 and sub == 0),
                            stop=(kt == 7 and sub == nsub - 1)), "qd")
                    for sub in range(nsub):
                        e.op("tensor", lambda t, hp=hp, kt=kt, sub=sub, par=par, pob=pob: t.matmul(
                            pob[:, sub * 65:sub * 65 + 65],
                            pt[:, 2 + par, sub * 128:(sub + 1) * 128],
                            svp[:, kt, 65 * (2 * hp + 1):65 * (2 * hp + 1) + 65],
                            start=(kt == 0 and sub == 0),
                            stop=(kt == 7 and sub == nsub - 1)), "qd")
                e.wait("vector", "qd")
                ha, hb = 2 * hp, 2 * hp + 1
                poa = c.PS[4 + 2 * (hp % 2)]
                pob = c.PS[5 + 2 * (hp % 2)]
                for sub in range(nsub):
                    if ks == 0:
                        e.op("vector", lambda v, sub=sub, ha=ha, poa=poa: v.tensor_copy(
                            oacc[:, sub, 65 * ha:65 * ha + 65], poa[:, sub * 65:sub * 65 + 65]))
                        e.op("vector", lambda v, sub=sub, hb=hb, pob=pob: v.tensor_copy(
                            oacc[:, sub, 65 * hb:65 * hb + 65], pob[:, sub * 65:sub * 65 + 65]))
                    else:
                        e.op("vector", lambda v, sub=sub, ha=ha, poa=poa: v.tensor_tensor(
                            oacc[:, sub, 65 * ha:65 * ha + 65],
                            oacc[:, sub, 65 * ha:65 * ha + 65],
                            poa[:, sub * 65:sub * 65 + 65], ALU.add))
                        e.op("vector", lambda v, sub=sub, hb=hb, pob=pob: v.tensor_tensor(
                            oacc[:, sub, 65 * hb:65 * hb + 65],
                            oacc[:, sub, 65 * hb:65 * hb + 65],
                            pob[:, sub * 65:sub * 65 + 65], ALU.add))
                e.op("vector", lambda v: v.tensor_copy(c.s_t1, c.s_t1), "ev")
            e.op("tensor", lambda t: t.nop(), "fz")
        e.wait("vector", "fz")
        stb = e.cnt["st"]
        for sub in range(nsub):
            if sub >= 2:
                e.wait("vector", "st", stb + 16 * (sub - 1))
            ov = oacc[:, sub, :].rearrange("p (h z) -> p h z", h=NH)
            pk = A(c, c.o_et + (sub % 2) * D, D).rearrange("p (h z) -> p h z", h=NH)
            for h in range(NH):
                e.vop("vector", lambda v, h=h, ov=ov: v.reciprocal(c.s_t0, ov[:, h, 64:65]))
                e.op("vector", lambda v, h=h, ov=ov, pk=pk: v.tensor_scalar(
                    pk[:, h, :], ov[:, h, 0:64], c.s_t0, None, ALU.mult))
            e.op("vector", lambda v: v.tensor_copy(c.s_t1, c.s_t1), "g3")
            e.wait("gpsimd", "g3")
            e.dma("gpsimd", lambda gp, sub=sub, qoff=qoff: gp.dma_start(
                out=d["oscr"][qoff + sub * 128:qoff + (sub + 1) * 128, :],
                in_=A(c, c.o_et + (sub % 2) * D, D)), "st")
        e.wait("tensor", "st")
        qoff += qn



def ph_A4(c):
    e, d = c.e, c.d
    o_wo = c.o_big
    for ch in range(8):
        e.dma("sync", lambda s, ch=ch: s.dma_start(
            out=A(c, o_wo + ch * D, D), in_=d["wo_t"][ch * 128:(ch + 1) * 128, :]), "wl")
    e.dma("sync", lambda s: s.dma_start(out=c.row0, in_=d["bo_row"]), "wl")
    e.wait("tensor", "wl")
    e.wait("vector", "wl")
    e.wait("scalar", "wl")

    xst = [A(c, c.o_xt, D), A(c, c.o_xt + D, D)]
    ost = [A(c, c.o_et, D), A(c, c.o_et + D, D)]
    att = A(c, c.o_pt, D)
    ofm = A3d(c, c.o_k2, 8, 128)
    ldb = e.cnt["ld"]
    g4b = e.cnt["g4"]
    for i in range(HT):
        if i >= 2:
            e.wait("sync", "g4", g4b + i - 1)
        e.dma("sync", lambda s, i=i: s.dma_start(
            out=ost[i % 2], in_=d["oscr"][i * 128:(i + 1) * 128, :]), "ld")
        e.dma("sync", lambda s, i=i: s.dma_start(
            out=xst[i % 2], in_=d["x_halo"][i * 128:(i + 1) * 128, :]), "ld")
        lv = ldb + 32 * (i + 1)
        e.wait("tensor", "ld", lv)
        for ch in range(8):
            transpose_tile(c, ost[i % 2][:, ch * 128:(ch + 1) * 128],
                           ofm[:, ch, :], c.PS[0], c.PS[1], 8 * i + ch)
        e.wait("tensor", "ev")
        for nh in range(2):
            pq = c.PS[2] if nh == 0 else c.PS[3]
            for kc in range(8):
                e.op("tensor", lambda t, nh=nh, kc=kc, pq=pq: t.matmul(
                    pq[:, 0:512], ofm[:, kc, :],
                    A(c, o_wo + kc * D + nh * 512, 512),
                    start=(kc == 0), stop=False), "mm")
            e.op("tensor", lambda t, nh=nh, pq=pq: t.matmul(
                pq[:, 0:512], c.onesr_s, c.row0[0:1, nh * 512:(nh + 1) * 512],
                start=False, stop=True), "mm")
            e.wait("vector", "mm")
            e.op("vector", lambda v, nh=nh, pq=pq: v.tensor_copy(
                att[:, nh * 512:(nh + 1) * 512], pq[:, 0:512]), "ev")
        e.wait("scalar", "ev")
        e.wait("vector", "ld", lv)
        e.wait("scalar", "ld", lv)
        ln_tile(c, att, att, double_ln=False)
        e.op("vector", lambda v, i=i: v.tensor_tensor(att, xst[i % 2], att, ALU.add), "g4")
        e.wait("gpsimd", "g4")
        e.dma("gpsimd", lambda gp, i=i: gp.dma_start(
            out=d["x1scr"][i * 128:(i + 1) * 128, :], in_=att), "st")
        e.wait("vector", "st")



def _quant_weight(c, name_w, nrows, o_stage, o_dst_bf16, sw_col, dw_col):
    e, d = c.e, c.d
    wcols = d[name_w].tensor.shape[1] if hasattr(d[name_w], "tensor") else None
    wcols = DFF if name_w == "w1_t" else D
    nt = nrows // 128
    stg = [A(c, o_stage, wcols), A(c, o_stage + wcols, wcols)]
    e.vop("vector", lambda v: v.memset(c.s_t0, 0.0))
    g2b = e.cnt["g2"]
    for t in range(nt):
        if t >= 2:
            e.wait("sync", "g2", g2b + t - 1)
        e.dma("sync", lambda s, t=t: s.dma_start(
            out=stg[t % 2], in_=d[name_w][t * 128:(t + 1) * 128, :]), "wl")
        e.wait("vector", "wl")
        for sub in range(wcols // D):
            e.vop("vector", lambda v, t=t, sub=sub: v.tensor_reduce(
                c.s_t1, stg[t % 2][:, sub * D:(sub + 1) * D], AX.X, ALU.add,
                apply_absolute_value=True))
            e.vop("vector", lambda v: v.tensor_tensor(c.s_t0, c.s_t0, c.s_t1, ALU.add))
        e.op("vector", lambda v: v.tensor_copy(c.s_t1, c.s_t1), inc="g2")
        e.wait("vector", "g2")
    e.op("vector", lambda v: v.tensor_copy(c.s_t1, c.s_t1), "g3")
    e.wait("gpsimd", "g3")
    e.dma("gpsimd", lambda gp: gp.dma_start(out=d["x2scr"][4:5, 0:128], in_=c.s_t0), "st")
    e.wait("sync", "st")
    e.dma("sync", lambda s: s.dma_start(out=c.statc[0:1, 48:176], in_=d["x2scr"][4:5, 0:128]), "ld")
    e.wait("vector", "ld")
    e.vop("vector", lambda v: v.tensor_reduce(c.s_t0[0:1, :], c.statc[0:1, 48:176], AX.X, ALU.add))
    e.vop("vector", lambda v: v.tensor_scalar(
        c.s_t0[0:1, :], c.s_t0[0:1, :], 1.0 / (nrows * wcols), None, ALU.mult))
    e.vop("vector", lambda v: v.tensor_scalar(
        c.s_t0[0:1, :], c.s_t0[0:1, :], 1e-5, None, ALU.max))
    e.vop("vector", lambda v: v.reciprocal(c.s_t1[0:1, :], c.s_t0[0:1, :]))
    e.op("vector", lambda v: v.tensor_copy(c.s_sq[0:1, :], c.s_sq[0:1, :]), "g1")
    e.wait("tensor", "g1")
    e.op("tensor", lambda t: t.matmul(c.PS[6][:, 0:1], c.onesr_s, c.s_t0[0:1, 0:1],
                                      start=True, stop=True), "mm")
    e.op("tensor", lambda t: t.matmul(c.PS[7][:, 0:1], c.onesr_s, c.s_t1[0:1, 0:1],
                                      start=True, stop=True), "mm")
    e.wait("vector", "mm")
    e.op("vector", lambda v: v.tensor_copy(dw_col, c.PS[6][:, 0:1]))
    e.op("vector", lambda v: v.tensor_copy(sw_col, c.PS[7][:, 0:1]), "g2")
    e.wait("vector", "g2")
    wq = c.arena[:, o_dst_bf16:o_dst_bf16 + nt * (wcols // 2)].bitcast(BF16)
    qstage = A(c, c.o_sq, D)
    g2b = e.cnt["g2"]
    for t in range(nt):
        if t >= 2:
            e.wait("sync", "g2", g2b + t - 1)
        e.dma("sync", lambda s, t=t: s.dma_start(
            out=stg[t % 2], in_=d[name_w][t * 128:(t + 1) * 128, :]), "wl")
        e.wait("vector", "wl")
        for hh in range(wcols // D):
            sl = stg[t % 2][:, hh * D:(hh + 1) * D]
            e.op("vector", lambda v, sl=sl: v.tensor_scalar(
                sl, sl, sw_col, None, ALU.mult))
            e.op("vector", lambda v, sl=sl: v.tensor_copy(
                qstage.bitcast(I32), sl))
            e.op("vector", lambda v, sl=sl: v.tensor_copy(sl, qstage.bitcast(I32)))
            e.op("vector", lambda v, sl=sl: v.tensor_scalar(sl, sl, 1.0, None, ALU.min))
            e.op("vector", lambda v, sl=sl: v.tensor_scalar(sl, sl, -1.0, None, ALU.max))
            e.op("vector", lambda v, t=t, hh=hh, sl=sl: v.tensor_copy(
                wq[:, (t * (wcols // D) + hh) * D:(t * (wcols // D) + hh + 1) * D], sl))
        e.op("vector", lambda v: v.tensor_copy(c.s_sq, c.s_sq), inc="g2")
        e.wait("vector", "g2")


def ph_B(c):
    c.o_sq = 49664
    do_b0 = "B0" in PHASES or "B" in PHASES
    do_ba = "BA" in PHASES or "B" in PHASES
    do_bb = "BB" in PHASES or "B" in PHASES
    e, d = c.e, c.d
    oW, oRP, oCB = 0, 16384, 28672
    oHP, oHC, oCL = 32768, 36864, 40960
    oXT, oQS, oXF, oWS = 45056, 47104, 48128, 48640
    sw1 = c.statc[:, 120:121]; dw1 = c.statc[:, 121:122]
    sw2 = c.statc[:, 122:123]; dw2 = c.statc[:, 123:124]
    flagl = c.statc[:, 124:125]; flagr = c.statc[:, 125:126]
    amax2_c = c.statc[:, 126:127]
    s1_c = c.statc[:, 127:128]

    if not do_b0:
        return
    _quant_weight(c, "w1_t", D, oRP, oW, sw1, dw1)
    e.barrier()
    for k in range(3):
        e.dma("sync", lambda s, k=k: s.dma_start(
            out=c.arena[0:1, oRP + k * DFF:oRP + (k + 1) * DFF],
            in_=d["convw"][k:k + 1, :]), "wl")
    e.dma("sync", lambda s: s.dma_start(
        out=c.arena[0:1, oCB:oCB + DFF], in_=d["convb_r"]), "wl")
    e.wait("tensor", "wl")
    for k in range(4):
        dst_off = (oRP + k * DFF) if k < 3 else oCB
        src = c.arena[0:1, dst_off:dst_off + DFF]
        for n in range(8):
            e.op("tensor", lambda t, k=k, n=n, src=src: t.matmul(
                c.PS[n % 2][:, 0:512], c.onesr_s, src[0:1, n * 512:(n + 1) * 512],
                start=True, stop=True), "mm")
            e.wait("vector", "mm")
            e.op("vector", lambda v, n=n, dst_off=dst_off: v.tensor_copy(
                A(c, dst_off + n * 512, 512), c.PS[n % 2][:, 0:512]), "ev")
            e.wait("tensor", "ev")
    e.op("vector", lambda v: v.memset(flagl, 1.0))
    e.op("vector", lambda v: v.memset(flagr, 1.0), "g1")
    e.wait("vector", "g1")
    e.vop("vector", lambda v: v.tensor_copy(flagl[0:1, :], c.flags_s[0:1, 0:1]))
    e.op("vector", lambda v: v.tensor_copy(c.s_t1, c.s_t1), "g3")
    e.wait("gpsimd", "g3")
    e.dma("gpsimd", lambda gp: gp.dma_start(out=d["x2scr"][5:6, 0:1], in_=c.flags_s[0:1, 1:2]), "st")
    e.wait("sync", "st")
    e.dma("sync", lambda s: s.dma_start(out=flagr[1:2, :], in_=d["x2scr"][5:6, 0:1]), "ld")
    e.wait("vector", "ld")
    e.op("vector", lambda v: v.tensor_copy(c.s_sq, c.s_sq), "g1")

    xst = [A(c, oXT, D), A(c, oXT + D, D)]
    w1q = c.arena[:, oW:oW + 16384].bitcast(BF16)
    xqf = c.arena[:, oXF:oXF + 512].bitcast(BF16)
    ldb = e.cnt["ld"]
    g4b = e.cnt["g4"]

    def conv_group(g):
        hsrc = A(c, oHP if g % 2 == 0 else oHC, DFF)
        for half in range(2):
            cl0 = A(c, oCL + 0, 2048)
            cl1 = A(c, oCL + 2048, 2048)
            off = half * 2048
            base = g * 128
            e.wait("sync", "g5")
            e.wait("sync", "st")
            if base == 0:
                e.dma("sync", lambda s: s.dma_start(
                    out=cl0[1:128, :], in_=d["hfscr"][0:127, off:off + 2048]), "ld")
                e.dma("sync", lambda s: s.dma_start(
                    out=cl0[0:1, :], in_=d["hfscr"][HS:HS + 1, off:off + 2048]), "ld")
            else:
                e.dma("sync", lambda s, base=base: s.dma_start(
                    out=cl0, in_=d["hfscr"][base - 1:base + 127, off:off + 2048]), "ld")
            e.dma("sync", lambda s, base=base: s.dma_start(
                out=cl1, in_=d["hfscr"][base + 1:base + 129, off:off + 2048]), "ld")
            e.wait("vector", "ld")
            hs = hsrc[:, off:off + 2048]
            e.op("vector", lambda v, hs=hs, off=off: v.tensor_tensor(
                hs, hs, A(c, oRP + DFF + off, 2048), ALU.mult))
            e.op("vector", lambda v, cl0=cl0, off=off: v.tensor_tensor(
                cl0, cl0, A(c, oRP + off, 2048), ALU.mult))
            e.op("vector", lambda v, cl1=cl1, off=off: v.tensor_tensor(
                cl1, cl1, A(c, oRP + 2 * DFF + off, 2048), ALU.mult))
            e.op("vector", lambda v, hs=hs, cl0=cl0: v.tensor_tensor(hs, hs, cl0, ALU.add))
            e.op("vector", lambda v, hs=hs, cl1=cl1: v.tensor_tensor(hs, hs, cl1, ALU.add))
            e.op("vector", lambda v, hs=hs, off=off: v.tensor_tensor(
                hs, hs, A(c, oCB + off, 2048), ALU.add), inc="g5")
        e.wait("scalar", "g5")
        e.op("scalar", lambda s, hsrc=hsrc: s.activation(hsrc, hsrc, AF.Silu), inc="ex")
        e.wait("vector", "ex")
        e.vop("vector", lambda v: v.memset(amax2_c, 0.0))
        for sub in range(4):
            e.vop("vector", lambda v, hsrc=hsrc, sub=sub: v.tensor_reduce(
                c.s_t1, hsrc[:, sub * D:(sub + 1) * D], AX.X, ALU.max))
            e.vop("vector", lambda v: v.tensor_tensor(amax2_c, amax2_c, c.s_t1, ALU.max))
            e.vop("vector", lambda v, hsrc=hsrc, sub=sub: v.tensor_reduce(
                c.s_t1, hsrc[:, sub * D:(sub + 1) * D], AX.X, ALU.min))
            e.vop("vector", lambda v: v.tensor_scalar(c.s_t1, c.s_t1, -1.0, None, ALU.mult))
            e.vop("vector", lambda v: v.tensor_tensor(amax2_c, amax2_c, c.s_t1, ALU.max))
        e.vop("vector", lambda v: v.tensor_scalar(amax2_c, amax2_c, 1e-5, None, ALU.max))
        e.vop("vector", lambda v: v.reciprocal(s1_c, amax2_c))
        e.vop("vector", lambda v: v.tensor_scalar(s1_c, s1_c, 127.0, None, ALU.mult))
        e.vop("vector", lambda v: v.tensor_scalar(amax2_c, amax2_c, 1.0 / 127.0, None, ALU.mult))
        e.op("vector", lambda v: v.tensor_copy(c.s_sq, c.s_sq), "g3")
        e.wait("gpsimd", "g3")
        e.dma("gpsimd", lambda gp, g=g: gp.dma_start(
            out=d["hfscr"][HS + 8 + g:HS + 9 + g, 0:128], in_=amax2_c), "st")
        for hh in range(4):
            sl = hsrc[:, hh * D:(hh + 1) * D]
            e.op("vector", lambda v, sl=sl: v.tensor_scalar(sl, sl, s1_c, None, ALU.mult))
            e.op("vector", lambda v, sl=sl: v.tensor_copy(A(c, oWS, D).bitcast(I32), sl))
            e.op("vector", lambda v, sl=sl: v.tensor_copy(sl, A(c, oWS, D).bitcast(I32)))
        e.op("vector", lambda v: v.tensor_copy(c.s_sq, c.s_sq), "g3")
        e.wait("gpsimd", "g3")
        e.dma("gpsimd", lambda gp, g=g, hsrc=hsrc: gp.dma_start(
            out=d["xq2scr"][g * 128:(g + 1) * 128, :], in_=hsrc), "st")
        e.wait("vector", "st")
        e.wait("sync", "st")
        e.op("sync", lambda s: s.nop(), "g5")

    if not do_ba:
        return
    e.op("sync", lambda s: s.nop(), "g5")
    for g in range(HT):
        if g >= 2:
            e.wait("sync", "g4", g4b + g - 1)
        e.dma("sync", lambda s, g=g: s.dma_start(
            out=xst[g % 2], in_=d["x1scr"][g * 128:(g + 1) * 128, :]), "ld")
        lv = e.cnt["ld"]
        e.wait("vector", "ld", lv)
        e.wait("scalar", "ld", lv)
        ln_tile(c, xst[g % 2], xst[g % 2], double_ln=False)
        e.vop("vector", lambda v, g=g: v.tensor_reduce(
            c.s_t0, xst[g % 2], AX.X, ALU.max))
        e.vop("vector", lambda v, g=g: v.tensor_reduce(
            c.s_t1, xst[g % 2], AX.X, ALU.min))
        e.vop("vector", lambda v: v.tensor_scalar(c.s_t1, c.s_t1, -1.0, None, ALU.mult))
        e.vop("vector", lambda v: v.tensor_tensor(c.s_t0, c.s_t0, c.s_t1, ALU.max))
        e.vop("vector", lambda v: v.tensor_scalar(c.s_t0, c.s_t0, 1e-5, None, ALU.max))
        e.vop("vector", lambda v: v.reciprocal(c.s_t1, c.s_t0))
        e.vop("vector", lambda v: v.tensor_scalar(c.s_t1, c.s_t1, 127.0, None, ALU.mult))
        e.vop("vector", lambda v: v.tensor_scalar(c.s_t0, c.s_t0, 1.0 / 127.0, None, ALU.mult))
        e.vop("vector", lambda v: v.tensor_tensor(c.s_a, c.s_t0, dw1, ALU.mult))
        xqf32 = A(c, oQS, D)
        e.op("vector", lambda v, g=g: v.tensor_scalar(
            xst[g % 2], xst[g % 2], c.s_t1, None, ALU.mult))
        e.op("vector", lambda v, g=g: v.tensor_copy(A(c, oWS, D).bitcast(I32), xst[g % 2]))
        e.op("vector", lambda v: v.tensor_copy(xqf32, A(c, oWS, D).bitcast(I32)), "qd")
        if DEBUG and g == 1:
            e.wait("gpsimd", "qd")
            e.dma("gpsimd", lambda gp: gp.dma_start(out=d["dbg_xq"], in_=xqf32), "st")
            e.dma("gpsimd", lambda gp, g=g: gp.dma_start(out=d["dbg_xn"], in_=xst[g % 2]), "st")
            e.op("vector", lambda v: v.tensor_copy(c.statc[:, 250:251], c.s_a))
            e.op("vector", lambda v: v.tensor_copy(c.statc[:, 251:252], c.s_t1), "g3")
            e.wait("gpsimd", "g3")
            e.dma("gpsimd", lambda gp: gp.dma_start(out=d["dbg_dq"], in_=c.statc[:, 248:252]), "st")
            e.wait("sync", "st")
        e.wait("tensor", "qd")
        for ch in range(8):
            transpose_tile(c, xqf32[:, ch * 128:(ch + 1) * 128],
                           c.arena[:, oXF + ch * 64:oXF + (ch + 1) * 64].bitcast(BF16),
                           c.PS[0], c.PS[1], ch)
        e.wait("tensor", "ev")
        hdst = A(c, oHP if g % 2 == 1 else oHC, DFF)
        hdst = A(c, oHP if g % 2 == 0 else oHC, DFF)
        for n in range(8):
            pq = c.PS[2] if n % 2 == 0 else c.PS[3]
            if n >= 2:
                e.wait("tensor", "ev", e.cnt["ev"] - 1)
            for kc in range(8):
                e.op("tensor", lambda t, n=n, kc=kc, pq=pq: t.matmul(
                    pq[:, 0:512],
                    c.arena[:, oXF + kc * 64:oXF + (kc + 1) * 64].bitcast(BF16),
                    w1q[:, kc * DFF + n * 512:kc * DFF + (n + 1) * 512],
                    start=(kc == 0), stop=(kc == 7)), "mm")
            e.wait("vector", "mm")
            e.op("vector", lambda v, n=n, pq=pq, hdst=hdst: v.tensor_scalar(
                hdst[:, n * 512:(n + 1) * 512], pq[:, 0:512], c.s_a, None, ALU.mult), "ev")
        if g == 0:
            e.wait("vector", "ev")
            e.op("vector", lambda v, hdst=hdst: v.tensor_scalar(
                hdst, hdst, flagl, None, ALU.mult))
        if g == 16:
            e.wait("vector", "ev")
            e.op("vector", lambda v, hdst=hdst: v.tensor_scalar(
                hdst, hdst, flagr, None, ALU.mult))
        e.op("vector", lambda v: v.tensor_copy(c.s_sq, c.s_sq), "g2")
        e.wait("gpsimd", "g2")
        e.dma("gpsimd", lambda gp, g=g, hdst=hdst: gp.dma_start(
            out=d["hfscr"][g * 128:(g + 1) * 128, :], in_=hdst), "st")
        if g >= 1:
            e.wait("vector", "st")
            conv_group(g - 1)
        e.op("tensor", lambda t: t.nop(), "g4")
    conv_group(HT - 1)

    e.barrier()
    if not do_bb:
        return
    _quant_weight(c, "w2_t", DFF, oRP, oW, sw2, dw2)
    e.barrier()
    w2q = c.arena[:, oW:oW + 16384].bitcast(BF16)
    xq2t = [A(c, oHP, DFF), A(c, oHC, DFF)]
    x2f = c.arena[:, oXF:oXF + 2048]
    ldb = e.cnt["ld"]
    g4b = e.cnt["g4"]
    for g in range(HT):
        if g >= 2:
            e.wait("sync", "g4", g4b + g - 1)
        e.dma("sync", lambda s, g=g: s.dma_start(
            out=xq2t[g % 2], in_=d["xq2scr"][g * 128:(g + 1) * 128, :]), "ld")
        e.dma("sync", lambda s, g=g: s.dma_start(
            out=amax2_c,
            in_=d["hfscr"][HS + 8 + g:HS + 9 + g, 0:128].rearrange("a b -> b a")), "ld")
        e.dma("sync", lambda s, g=g: s.dma_start(
            out=xst[g % 2], in_=d["x1scr"][g * 128:(g + 1) * 128, :]), "ld")
        lv = e.cnt["ld"]
        e.wait("vector", "ld", lv)
        e.vop("vector", lambda v: v.tensor_tensor(c.s_a, amax2_c, dw2, ALU.mult))
        e.wait("tensor", "ld", lv)
        e.wait("tensor", "g4", e.cnt["g4"])
        for ch in range(32):
            transpose_tile(c, xq2t[g % 2][:, ch * 128:(ch + 1) * 128],
                           c.arena[:, oXF + ch * 64:oXF + (ch + 1) * 64].bitcast(BF16),
                           c.PS[0], c.PS[1], ch)
        e.wait("tensor", "ev")
        for n in range(2):
            pq = c.PS[2] if n == 0 else c.PS[3]
            for kc in range(32):
                e.op("tensor", lambda t, n=n, kc=kc, pq=pq: t.matmul(
                    pq[:, 0:512],
                    c.arena[:, oXF + kc * 64:oXF + (kc + 1) * 64].bitcast(BF16),
                    w2q[:, kc * D + n * 512:kc * D + (n + 1) * 512],
                    start=(kc == 0), stop=(kc == 31)), "mm")
            e.wait("vector", "mm")
            e.op("vector", lambda v, n=n, pq=pq: v.tensor_scalar(
                A(c, oCL, D)[:, n * 512:(n + 1) * 512], pq[:, 0:512],
                c.s_a, None, ALU.mult), "ev")
        yt = A(c, oCL, D)
        e.wait("scalar", "ev")
        e.wait("vector", "ev")
        e.wait("scalar", "ld", lv)
        ln_tile(c, yt, yt, double_ln=False)
        e.op("vector", lambda v, g=g: v.tensor_tensor(yt, xst[g % 2], yt, ALU.add), "g2")
        e.wait("gpsimd", "g2")
        if g == 0:
            e.dma("gpsimd", lambda gp: gp.dma_start(
                out=d["x2scr"][0:127, :], in_=yt[1:128, :]), "st")
        elif g == 16:
            e.dma("gpsimd", lambda gp: gp.dma_start(
                out=d["x2scr"][2047:2048, :], in_=yt[0:1, :]), "st")
        else:
            e.dma("gpsimd", lambda gp, g=g: gp.dma_start(
                out=d["x2scr"][g * 128 - 1:g * 128 + 127, :], in_=yt), "st")
        e.wait("vector", "st")
        e.wait("tensor", "g2")
        e.op("tensor", lambda t: t.nop(), "g4")


def ph_tail_x2(c):
    e, d = c.e, c.d
    e.wait("gpsimd", "st")
    e.dma("gpsimd", lambda gp: gp.dma_start(out=d["y_out"], in_=d["x2scr"][0:TPC, :]), "st")
    _dbg_dumps(c)



def ph_C(c):
    e, d = c.e, c.d
    oW4, oBR, oXT3 = 0, 32768, 36864
    oXN, oFM = 38912, 39936
    oQ, oK, oV = 40960, 41984, 43008
    oQT, oKT = 44032, 45056
    oP, oPT, oAT, oRZ = 46080, 46336, 46592, 47616
    for wi, nm in enumerate(["l1wq", "l1wk", "l1wv", "l1wo"]):
        for ch in range(8):
            e.dma("sync", lambda s, wi=wi, nm=nm, ch=ch: s.dma_start(
                out=A(c, oW4 + wi * 8192 + ch * 1024, D),
                in_=d[nm][ch * 128:(ch + 1) * 128, :]), "wl")
    for bi in range(4):
        if bi > 0:
            e.wait("sync", "mm")
        e.dma("sync", lambda s, bi=bi: s.dma_start(
            out=c.row0, in_=d["l1b4"][bi:bi + 1, :]), "ld")
        e.wait("tensor", "ld")
        e.wait("tensor", "ev")
        for n in range(2):
            e.op("tensor", lambda t, n=n: t.matmul(
                c.PS[2 + n][:, 0:512], c.onesr_s, c.row0[0:1, n * 512:(n + 1) * 512],
                start=True, stop=True), "mm")
        e.wait("vector", "mm")
        for n in range(2):
            e.op("vector", lambda v, bi=bi, n=n: v.tensor_copy(
                A(c, oBR + bi * D + n * 512, 512), c.PS[2 + n][:, 0:512]), "ev")
    e.wait("tensor", "wl")
    e.wait("tensor", "ev")

    xst = [A(c, oXT3, D), A(c, oXT3 + D, D)]
    xn = A(c, oXN, D)
    g4b = e.cnt["g4"]
    for j in range(NT):
        if j >= 2:
            e.wait("sync", "g4", g4b + j - 1)
        e.dma("sync", lambda s, j=j: s.dma_start(
            out=xst[j % 2], in_=d["x2scr"][j * 128:(j + 1) * 128, :]), "ld")
        lv = e.cnt["ld"]
        e.wait("vector", "ld", lv)
        e.wait("scalar", "ld", lv)
        ln_tile(c, xst[j % 2], xn, double_ln=False)
        e.wait("tensor", "xr")
        e.wait("tensor", "ev")
        for ch in range(8):
            transpose_tile(c, xn[:, ch * 128:(ch + 1) * 128],
                           A(c, oFM + ch * 128, 128), c.PS[0], c.PS[1], ch)
        e.wait("tensor", "ev")
        for wi, dsto in [(0, oQ), (1, oK), (2, oV)]:
            for n in range(2):
                pq = c.PS[2 + n]
                for ch in range(8):
                    e.op("tensor", lambda t, wi=wi, n=n, ch=ch, pq=pq: t.matmul(
                        pq[:, 0:512], A(c, oFM + ch * 128, 128),
                        A(c, oW4 + wi * 8192 + ch * 1024 + n * 512, 512),
                        start=(ch == 0), stop=(ch == 7)), "mm")
                e.wait("vector", "mm")
                e.op("vector", lambda v, wi=wi, n=n, dsto=dsto, pq=pq: v.tensor_tensor(
                    A(c, dsto + n * 512, 512), pq[:, 0:512],
                    A(c, oBR + wi * D + n * 512, 512), ALU.add), "ev")
            e.wait("tensor", "ev")
        for i in range(16):
            src = A(c, (oQ if i < 8 else oK) + (i % 8) * 128, 128)
            dst = A(c, (oQT if i < 8 else oKT) + (i % 8) * 128, 128)
            transpose_tile(c, src, dst, c.PS[0], c.PS[1], i)
        e.wait("tensor", "ev")
        g2b = e.cnt["g2"]; exb = e.cnt["ex"]; qdb = e.cnt["qd"]
        g5b = e.cnt["g5"]; evb = e.cnt["ev"]
        for h in range(NH):
            ch, sub = h // 2, h % 2
            par = h % 2
            ps_s = c.PS[4 + par]
            if h >= 2:
                e.wait("tensor", "ex", exb + h - 1)
            qv = c.arena[64 * sub:64 * sub + 64, oQT + ch * 128:oQT + (ch + 1) * 128]
            kv = c.arena[64 * sub:64 * sub + 64, oKT + ch * 128:oKT + (ch + 1) * 128]
            if sub == 0:
                e.op("tensor", lambda t, qv=qv, kv=kv, ps_s=ps_s: t.matmul(
                    ps_s[:, 0:128], qv, kv, start=True, stop=True), "g2")
            else:
                e.op("tensor", lambda t, qv=qv, kv=kv, ps_s=ps_s: t.matmul(
                    ps_s[:, 0:128], qv, kv, start=True, stop=True,
                    tile_position=(64, 0)), "g2")
            e.wait("vector", "g2", g2b + h + 1)
            e.vop("vector", lambda v, ps_s=ps_s: v.tensor_reduce(
                c.s_t0, ps_s[:, 0:128], AX.X, ALU.max))
            e.op("vector", lambda v: v.tensor_scalar(
                c.s_t0, c.s_t0, -0.125, None, ALU.mult), "g3")
            e.wait("scalar", "g3")
            if h >= 2:
                e.wait("scalar", "g5", g5b + h - 1)
                e.wait("scalar", "qd", qdb + h - 1)
            pbuf = A(c, oP + par * 128, 128)
            e.op("scalar", lambda s, ps_s=ps_s, pbuf=pbuf: s.activation(
                pbuf, ps_s[:, 0:128], AF.Exp, bias=c.s_t0, scale=c.eighth_c), inc="ex")
            e.wait("vector", "ex", exb + h + 1)
            e.vop("vector", lambda v, pbuf=pbuf: v.tensor_reduce(
                c.s_t1, pbuf, AX.X, ALU.add))
            e.vop("vector", lambda v: v.reciprocal(c.s_t1, c.s_t1))
            e.op("vector", lambda v, h=h: v.tensor_copy(
                c.arena[:, oRZ + h:oRZ + h + 1], c.s_t1), "g5")
            ptb = c.PS[0 + par]
            e.wait("tensor", "ex", exb + h + 1)
            if h >= 2:
                e.wait("tensor", "ev", evb + h - 1)
            e.op("tensor", lambda t, pbuf=pbuf, ptb=ptb: t.transpose(
                ptb[:, 0:128], pbuf, c.ident_s), "qd")
            e.wait("vector", "qd", qdb + h + 1)
            e.wait("vector", "mm")
            ptbuf = A(c, oPT + par * 128, 128)
            e.op("vector", lambda v, ptb=ptb, ptbuf=ptbuf: v.tensor_copy(
                ptbuf, ptb[:, 0:128]), "ev")
            e.wait("tensor", "ev", evb + h + 1)
            pvb = c.PS[6 + h // 8]
            e.op("tensor", lambda t, h=h, ptbuf=ptbuf, pvb=pvb: t.matmul(
                pvb[:, (h % 8) * 64:(h % 8) * 64 + 64], ptbuf,
                A(c, oV + h * 64, 64), start=True, stop=True), "mm")
        e.wait("vector", "mm")
        for h in range(NH):
            pvb = c.PS[6 + h // 8]
            e.op("vector", lambda v, h=h, pvb=pvb: v.tensor_scalar(
                A(c, oXN + h * 64, 64), pvb[:, (h % 8) * 64:(h % 8) * 64 + 64],
                c.arena[:, oRZ + h:oRZ + h + 1], None, ALU.mult), "ev")
        e.wait("tensor", "ev")
        for chh in range(8):
            transpose_tile(c, A(c, oXN + chh * 128, 128),
                           A(c, oFM + chh * 128, 128), c.PS[0], c.PS[1], chh)
        e.wait("tensor", "ev")
        for n in range(2):
            pq = c.PS[2 + n]
            for chh in range(8):
                e.op("tensor", lambda t, n=n, chh=chh, pq=pq: t.matmul(
                    pq[:, 0:512], A(c, oFM + chh * 128, 128),
                    A(c, oW4 + 3 * 8192 + chh * 1024 + n * 512, 512),
                    start=(chh == 0), stop=(chh == 7)), "mm")
            e.wait("vector", "mm")
            e.op("vector", lambda v, n=n, pq=pq: v.tensor_tensor(
                A(c, oAT + n * 512, 512), pq[:, 0:512],
                A(c, oBR + 3 * D + n * 512, 512), ALU.add), "ev")
        att = A(c, oAT, D)
        e.wait("scalar", "ev")
        e.wait("vector", "ev")
        ln_tile(c, att, att, double_ln=False)
        e.op("vector", lambda v, j=j: v.tensor_tensor(att, xst[j % 2], att, ALU.add), "g2")
        e.wait("gpsimd", "g2")
        e.dma("gpsimd", lambda gp, j=j: gp.dma_start(
            out=d["x3scr"][j * 128:(j + 1) * 128, :], in_=att), "st")
        e.wait("vector", "st")
        e.wait("tensor", "g2")
        e.op("tensor", lambda t: t.nop(), "g4")


def ph_tail_x3(c):
    e, d = c.e, c.d
    e.wait("gpsimd", "st")
    e.dma("gpsimd", lambda gp: gp.dma_start(out=d["y_out"], in_=d["x3scr"][0:TPC, :]), "st")
    _dbg_dumps(c)



oRT = 49472
oROW = 49664


def _rtc(c, k, j):
    return c.arena[:, oRT + k * 16 + j:oRT + k * 16 + j + 1]


def _rrow(c, k, n=8):
    return c.arena[0:1, oROW + k * 8:oROW + k * 8 + n]


def ph_D0(c):
    e, d = c.e, c.d
    oFMD, oWR, oXTD = 0, 1024, 2048
    lg = c.statc[:, 48:56]
    eq0 = c.statc[:, 16:24]; eq1 = c.statc[:, 24:32]
    m8 = c.statc[:, 32:40]; t8 = c.statc[:, 40:48]
    b8 = c.statc[:, 56:64]; r8 = c.statc[:, 64:72]
    tmp8 = c.statc[:, 72:80]; bov8 = c.statc[:, 80:88]
    ic0 = c.statc[:, 250:251]; ic1 = c.statc[:, 251:252]
    runb = _rrow(c, 0); runov = _rrow(c, 1)

    for ch in range(8):
        e.dma("sync", lambda s, ch=ch: s.dma_start(
            out=A(c, oWR + ch * 8, 8), in_=d["wr_t"][ch * 128:(ch + 1) * 128, :]), "ld")
    e.op("vector", lambda v: v.memset(runb, 0.0))
    e.op("vector", lambda v: v.memset(runov, 0.0), "g3")

    xst = [A(c, oXTD, D), A(c, oXTD + D, D)]
    g4b = e.cnt["g4"]
    fzb = e.cnt["fz"]
    for j in range(NT):
        if j >= 2:
            e.wait("sync", "g4", g4b + j - 1)
        e.dma("sync", lambda s, j=j: s.dma_start(
            out=xst[j % 2], in_=d["x3scr"][j * 128:(j + 1) * 128, :]), "ld")
        lv = e.cnt["ld"]
        e.wait("vector", "ld", lv)
        e.wait("scalar", "ld", lv)
        ln_tile(c, xst[j % 2], xst[j % 2], double_ln=False)
        e.wait("gpsimd", "xr")
        e.dma("gpsimd", lambda gp, j=j: gp.dma_start(
            out=d["xn3scr"][j * 128:(j + 1) * 128, :], in_=xst[j % 2]), "st")
        e.wait("tensor", "xr")
        e.wait("tensor", "ev")
        for ch in range(8):
            transpose_tile(c, xst[j % 2][:, ch * 128:(ch + 1) * 128],
                           A(c, oFMD + ch * 128, 128), c.PS[0], c.PS[1], ch)
        e.wait("tensor", "ev")
        for ch in range(8):
            e.op("tensor", lambda t, ch=ch: t.matmul(
                c.PS[2][:, 0:8], A(c, oFMD + ch * 128, 128), A(c, oWR + ch * 8, 8),
                start=(ch == 0), stop=(ch == 7)), "mm")
        e.wait("tensor", "fz", fzb + j)
        e.op("tensor", lambda t: t.matmul(
            c.PS[2][:, 8:16], c.onesr_s, runb, start=True, stop=True), "mm")
        e.wait("vector", "mm")
        e.op("vector", lambda v: v.tensor_copy(lg, c.PS[2][:, 0:8]))
        e.op("vector", lambda v: v.tensor_copy(b8, c.PS[2][:, 8:16]), "ev")
        e.wait("vector", "ev")
        e.vop("vector", lambda v: v.tensor_reduce(c.s_t0, lg, AX.X, ALU.max))
        e.op("vector", lambda v: v.tensor_scalar(
            c.s_t0, c.s_t0, -1.0, None, ALU.mult), "g3")
        e.wait("scalar", "g3")
        e.op("scalar", lambda s: s.activation(lg, lg, AF.Exp, bias=c.s_t0, scale=1.0), inc="ex")
        e.wait("vector", "ex")
        e.vop("vector", lambda v: v.tensor_reduce(c.s_t1, lg, AX.X, ALU.add))
        e.vop("vector", lambda v: v.reciprocal(c.s_t1, c.s_t1))
        e.vop("vector", lambda v: v.tensor_scalar(lg, lg, c.s_t1, None, ALU.mult))
        e.vop("vector", lambda v: v.tensor_reduce(c.s_t0, lg, AX.X, ALU.max))
        e.vop("vector", lambda v, j=j: v.tensor_copy(_rtc(c, 2, j), c.s_t0))
        e.vop("vector", lambda v: v.tensor_scalar(eq0, lg, c.s_t0, None, ALU.is_equal))
        e.vop("vector", lambda v: v.tensor_tensor(m8, eq0, c.idx8_s, ALU.mult))
        e.vop("vector", lambda v: v.tensor_scalar(tmp8, eq0, -1e9, 1e9, ALU.mult, ALU.add))
        e.vop("vector", lambda v: v.tensor_tensor(m8, m8, tmp8, ALU.add))
        e.vop("vector", lambda v: v.tensor_reduce(c.s_t1, m8, AX.X, ALU.min))
        e.vop("vector", lambda v, j=j: v.tensor_copy(_rtc(c, 0, j), c.s_t1))
        e.vop("vector", lambda v: v.tensor_scalar(eq0, c.idx8_s, c.s_t1, None, ALU.is_equal))
        e.vop("vector", lambda v: v.tensor_scalar(tmp8, eq0, -2.0, None, ALU.mult))
        e.vop("vector", lambda v: v.tensor_tensor(t8, lg, tmp8, ALU.add))
        e.vop("vector", lambda v: v.tensor_reduce(c.s_t0, t8, AX.X, ALU.max))
        e.vop("vector", lambda v, j=j: v.tensor_copy(_rtc(c, 3, j), c.s_t0))
        e.vop("vector", lambda v: v.tensor_scalar(eq1, t8, c.s_t0, None, ALU.is_equal))
        e.vop("vector", lambda v: v.tensor_tensor(m8, eq1, c.idx8_s, ALU.mult))
        e.vop("vector", lambda v: v.tensor_scalar(tmp8, eq1, -1e9, 1e9, ALU.mult, ALU.add))
        e.vop("vector", lambda v: v.tensor_tensor(m8, m8, tmp8, ALU.add))
        e.vop("vector", lambda v: v.tensor_reduce(c.s_t1, m8, AX.X, ALU.min))
        e.vop("vector", lambda v, j=j: v.tensor_copy(_rtc(c, 1, j), c.s_t1))
        e.vop("vector", lambda v: v.tensor_scalar(eq1, c.idx8_s, c.s_t1, None, ALU.is_equal))
        e.op("vector", lambda v: v.tensor_tensor(t8, eq0, eq1, ALU.add), "g3")
        e.wait("tensor", "g3")
        e.op("tensor", lambda t: t.matmul(
            c.PS[3][:, 0:8], c.lstrict_s, t8, start=True, stop=True), "mm")
        e.op("tensor", lambda t: t.matmul(
            c.PS[3][0:1, 8:16], c.onec, t8, start=True, stop=True), "mm")
        e.wait("vector", "mm")
        e.op("vector", lambda v: v.tensor_copy(r8, c.PS[3][:, 0:8]), "ev")
        e.vop("vector", lambda v: v.tensor_tensor(
            runb, runb, c.PS[3][0:1, 8:16], ALU.add))
        e.op("vector", lambda v: v.tensor_copy(c.s_sq, c.s_sq), "fz")
        e.vop("vector", lambda v: v.tensor_tensor(t8, r8, b8, ALU.add))
        for k in range(2):
            eqk = eq0 if k == 0 else eq1
            e.vop("vector", lambda v, eqk=eqk: v.tensor_tensor(tmp8, t8, eqk, ALU.mult))
            e.vop("vector", lambda v: v.tensor_reduce(c.s_t0, tmp8, AX.X, ALU.add))
            e.vop("vector", lambda v, j=j, k=k: v.tensor_copy(_rtc(c, 4 + k, j), c.s_t0))
            e.vop("vector", lambda v, j=j, k=k: v.tensor_scalar(
                c.s_sum, _rtc(c, k, j), 2048.0, None, ALU.mult))
            e.vop("vector", lambda v: v.tensor_tensor(c.s_sum, c.s_sum, c.s_t0, ALU.add))
            e.vop("vector", lambda v, j=j, k=k: v.tensor_copy(_rtc(c, 8 + k, j), c.s_sum))
            e.vop("vector", lambda v, k=k: v.tensor_copy(
                (ic0 if k == 0 else ic1).bitcast(I32), c.s_sum))
        e.op("vector", lambda v: v.tensor_copy(c.s_sq, c.s_sq), "g3")
        e.wait("gpsimd", "g3")
        e.wait("gpsimd", "ld", lv)
        e.dma("gpsimd", lambda gp, j=j: gp.indirect_dma_start(
            d["xsscr"], IndirectOffsetOnAxis(ap=ic0.bitcast(U32), axis=0),
            xst[j % 2], None, bounds_check=NSLOT - 1, oob_is_err=False), "st")
        e.dma("gpsimd", lambda gp, j=j: gp.indirect_dma_start(
            d["xsscr"], IndirectOffsetOnAxis(ap=ic1.bitcast(U32), axis=0),
            xst[j % 2], None, bounds_check=NSLOT - 1, oob_is_err=False), "st")
        e.wait("vector", "st")
        e.wait("sync", "st")
        e.wait("tensor", "st")
        e.op("tensor", lambda t: t.nop(), "g4")


def _rep_row(c, name, row_slc, width, dsto):
    e, d = c.e, c.d
    rowstg = c.arena[0:1, 43008:43008 + width]
    e.wait("sync", "mm")
    e.dma("sync", lambda s: s.dma_start(out=rowstg, in_=d[name][row_slc, 0:width]), "ld")
    e.wait("tensor", "ld")
    e.wait("tensor", "ev")
    for n in range(width // 512):
        pq = c.PS[2 + n % 2]
        if n >= 2:
            e.wait("tensor", "ev", e.cnt["ev"] - 1)
        e.op("tensor", lambda t, n=n, pq=pq: t.matmul(
            pq[:, 0:512], c.onesr_s, rowstg[0:1, n * 512:(n + 1) * 512],
            start=True, stop=True), "mm")
        e.wait("vector", "mm")
        e.op("vector", lambda v, n=n, pq=pq: v.tensor_copy(
            A(c, dsto + n * 512, 512), pq[:, 0:512]), "ev")
    e.wait("tensor", "ev")
    e.wait("vector", "ev")


def ph_D1(c):
    e, d = c.e, c.d
    oWS1, oWS2, oB1R, oB2R = 0, 16384, 32768, 36864
    oXTD, oFMD, oHS, oHT, oYT = 37888, 39936, 40960, 43008, 45056
    e.dma("sync", lambda s: s.dma_start(
        out=c.arena[0:1, oB1R:oB1R + DFF], in_=d["bs1_row"]), "wl")
    e.dma("sync", lambda s: s.dma_start(
        out=c.arena[0:1, oB2R:oB2R + D], in_=d["bs2_row"]), "wl")
    for half in range(2):
        e.barrier()
        for ch in range(8):
            e.dma("sync", lambda s, ch=ch, half=half: s.dma_start(
                out=A(c, oWS1 + ch * 2048, 2048),
                in_=d["ws1_t"][ch * 128:(ch + 1) * 128,
                               half * 2048:(half + 1) * 2048]), "wl")
        for fc in range(16):
            e.dma("sync", lambda s, fc=fc, half=half: s.dma_start(
                out=A(c, oWS2 + fc * 1024, D),
                in_=d["ws2_t"][half * 2048 + fc * 128:half * 2048 + (fc + 1) * 128, :]),
                "wl")
        e.wait("tensor", "wl")
        xst = [A(c, oXTD, D), A(c, oXTD + D, D)]
        g4b = e.cnt["g4"]
        for j in range(NT):
            if j >= 2:
                e.wait("sync", "g4", g4b + j - 1)
            e.dma("sync", lambda s, j=j: s.dma_start(
                out=xst[j % 2], in_=d["xn3scr"][j * 128:(j + 1) * 128, :]), "ld")
            lv = e.cnt["ld"]
            e.wait("tensor", "ld", lv)
            e.wait("tensor", "ev")
            for ch in range(8):
                transpose_tile(c, xst[j % 2][:, ch * 128:(ch + 1) * 128],
                               A(c, oFMD + ch * 128, 128), c.PS[0], c.PS[1], ch)
            e.wait("tensor", "ev")
            for n in range(4):
                pq = c.PS[2 + n % 2]
                if n >= 2:
                    e.wait("tensor", "ev", e.cnt["ev"] - 1)
                for ch in range(8):
                    e.op("tensor", lambda t, n=n, ch=ch, pq=pq: t.matmul(
                        pq[:, 0:512], A(c, oFMD + ch * 128, 128),
                        A(c, oWS1 + ch * 2048 + n * 512, 512),
                        start=(ch == 0), stop=False), "mm")
                e.op("tensor", lambda t, n=n, half=half, pq=pq: t.matmul(
                    pq[:, 0:512], c.onesr_s,
                    c.arena[0:1, oB1R + half * 2048 + n * 512:
                            oB1R + half * 2048 + (n + 1) * 512],
                    start=False, stop=True), "mm")
                e.wait("vector", "mm")
                e.op("vector", lambda v, n=n, pq=pq: v.tensor_copy(
                    A(c, oHS + n * 512, 512), pq[:, 0:512]), "ev")
            e.wait("scalar", "ev")
            e.op("scalar", lambda s: s.activation(
                A(c, oHS, 2048), A(c, oHS, 2048), AF.Gelu_apprx_tanh), inc="ex")
            e.wait("tensor", "ex")
            for fc in range(16):
                transpose_tile(c, A(c, oHS + fc * 128, 128),
                               A(c, oHT + fc * 128, 128), c.PS[0], c.PS[1], fc)
            e.wait("tensor", "ev")
            for n in range(2):
                pq = c.PS[4 + n]
                for fc in range(16):
                    e.op("tensor", lambda t, n=n, fc=fc, pq=pq: t.matmul(
                        pq[:, 0:512], A(c, oHT + fc * 128, 128),
                        A(c, oWS2 + fc * 1024 + n * 512, 512),
                        start=(fc == 0), stop=(half == 1 and fc == 15)), "mm")
                if half == 0:
                    e.op("tensor", lambda t, n=n, pq=pq: t.matmul(
                        pq[:, 0:512], c.onesr_s,
                        c.arena[0:1, oB2R + n * 512:oB2R + (n + 1) * 512],
                        start=False, stop=True), "mm")
            if half == 1:
                e.wait("sync", "ev")
                e.dma("sync", lambda s, j=j: s.dma_start(
                    out=A(c, oYT, D), in_=d["sascr"][j * 128:(j + 1) * 128, :]), "ld")
            lv2 = e.cnt["ld"]
            e.wait("vector", "mm")
            e.wait("vector", "ld", lv2)
            for n in range(2):
                pq = c.PS[4 + n]
                if half == 0:
                    e.op("vector", lambda v, n=n, pq=pq, j=j: v.tensor_copy(
                        xst[j % 2][:, n * 512:(n + 1) * 512], pq[:, 0:512]), "ev")
                else:
                    e.op("vector", lambda v, n=n, pq=pq, j=j: v.tensor_tensor(
                        xst[j % 2][:, n * 512:(n + 1) * 512], pq[:, 0:512],
                        A(c, oYT + n * 512, 512), ALU.add), "ev")
            e.op("vector", lambda v: v.tensor_copy(c.s_sq, c.s_sq), "g2")
            e.wait("gpsimd", "g2")
            e.dma("gpsimd", lambda gp, j=j: gp.dma_start(
                out=d["sascr"][j * 128:(j + 1) * 128, :], in_=xst[j % 2]), "st")
            e.wait("vector", "st")
            e.wait("tensor", "st")
            e.op("tensor", lambda t: t.nop(), "g4")


def _d2_tile1(c, sb):
    e, d = c.e, c.d
    oXG, oFMD, oHS = 36864, 37888, 38912
    b1row = c.arena[0:1, 43008:43008 + DFF]
    e.wait("sync", "st")
    e.wait("sync", "ld")
    e.dma("sync", lambda s, sb=sb: s.dma_start(
        out=A(c, oXG, D), in_=d["xsscr"][sb:sb + 128, :]), "ld")
    lv = e.cnt["ld"]
    e.wait("tensor", "ld", lv)
    e.wait("tensor", "ev")
    for ch in range(8):
        transpose_tile(c, c.arena[:, oXG + ch * 128:oXG + (ch + 1) * 128],
                       A(c, oFMD + ch * 128, 128), c.PS[0], c.PS[1], ch)
    e.wait("tensor", "ev")
    for n in range(8):
        pq = c.PS[2 + n % 2]
        if n >= 2:
            e.wait("tensor", "ev", e.cnt["ev"] - 1)
        for ch in range(8):
            e.op("tensor", lambda t, n=n, ch=ch, pq=pq: t.matmul(
                pq[:, 0:512], A(c, oFMD + ch * 128, 128),
                A(c, ch * DFF + n * 512, 512),
                start=(ch == 0), stop=False), "mm")
        e.op("tensor", lambda t, n=n, pq=pq, b1row=b1row: t.matmul(
            pq[:, 0:512], c.onesr_s, b1row[0:1, n * 512:(n + 1) * 512],
            start=False, stop=True), "mm")
        e.wait("vector", "mm")
        e.op("vector", lambda v, n=n, pq=pq: v.tensor_copy(
            A(c, oHS + n * 512, 512), pq[:, 0:512]), "ev")
    e.wait("scalar", "ev")
    e.op("scalar", lambda s: s.activation(
        A(c, oHS, DFF), A(c, oHS, DFF), AF.Gelu_apprx_tanh), inc="ex")
    e.wait("gpsimd", "ex")
    e.dma("gpsimd", lambda gp, sb=sb: gp.dma_start(
        out=d["hmscr"][sb:sb + 128, :], in_=A(c, oHS, DFF)), "st")
    e.wait("scalar", "st")
    e.wait("vector", "st")


def _d2_tile2(c, sb):
    e, d = c.e, c.d
    oHS2, oHT2, oYT = 33792, 37888, 41984
    b2row = c.arena[0:1, 43008:43008 + D]
    e.wait("sync", "st")
    e.wait("sync", "ld")
    e.dma("sync", lambda s, sb=sb: s.dma_start(
        out=A(c, oHS2, DFF), in_=d["hmscr"][sb:sb + 128, :]), "ld")
    lv = e.cnt["ld"]
    e.wait("tensor", "ld", lv)
    e.wait("tensor", "ev")
    for fc in range(32):
        transpose_tile(c, A(c, oHS2 + fc * 128, 128),
                       A(c, oHT2 + fc * 128, 128), c.PS[0], c.PS[1], fc)
    e.wait("tensor", "ev")
    for n in range(2):
        pq = c.PS[4 + n]
        for fc in range(32):
            e.op("tensor", lambda t, n=n, fc=fc, pq=pq: t.matmul(
                pq[:, 0:512], A(c, oHT2 + fc * 128, 128),
                A(c, fc * D + n * 512, 512),
                start=(fc == 0), stop=False), "mm")
        e.op("tensor", lambda t, n=n, pq=pq, b2row=b2row: t.matmul(
            pq[:, 0:512], c.onesr_s, b2row[0:1, n * 512:(n + 1) * 512],
            start=False, stop=True), "mm")
    e.wait("vector", "mm")
    for n in range(2):
        pq = c.PS[4 + n]
        e.op("vector", lambda v, n=n, pq=pq: v.tensor_copy(
            A(c, oYT + n * 512, 512), pq[:, 0:512]), "ev")
    e.op("vector", lambda v: v.tensor_copy(c.s_sq, c.s_sq), "g2")
    e.wait("gpsimd", "g2")
    e.dma("gpsimd", lambda gp, sb=sb: gp.dma_start(
        out=d["yscr"][sb:sb + 128, :], in_=A(c, oYT, D)), "st")
    e.wait("vector", "st")
    e.wait("tensor", "st")


def _d2_pool_w(c, T, wname, nw, nb):
    e, d = c.e, c.d
    etr = c.arena[0:1, oROW + 40:oROW + 47]
    ecol = c.statc[:, 252:253]
    ic0 = c.statc[:, 250:251]
    e.wait("tensor", "g3")
    e.wait("tensor", "ev")
    e.op("tensor", lambda t, T=T: t.matmul(
        c.PS[2][:, 0:1], c.onesr_s, etr[0:1, T:T + 1], start=True, stop=True), "mm")
    e.wait("vector", "mm")
    e.op("vector", lambda v: v.tensor_copy(ecol, c.PS[2][:, 0:1]), "ev")
    e.wait("vector", "ev")
    e.vop("vector", lambda v: v.tensor_scalar(c.s_t0, ecol, 128.0, None, ALU.mult))
    e.vop("vector", lambda v: v.tensor_tensor(c.s_t0, c.s_t0, c.tokid_s[:, 0:1], ALU.add))
    e.vop("vector", lambda v: v.tensor_copy(ic0.bitcast(I32), c.s_t0))
    e.op("vector", lambda v: v.tensor_copy(c.s_sq, c.s_sq), "g3")
    e.wait("gpsimd", "g3")
    e.dma("gpsimd", lambda gp, wname=wname, nw=nw, nb=nb: gp.indirect_dma_start(
        c.arena[:, 0:nw + nb], None, d[wname],
        IndirectOffsetOnAxis(ap=ic0.bitcast(U32), axis=0),
        bounds_check=NEXP * 128 - 1, oob_is_err=False), "wl")
    e.wait("tensor", "wl")
    e.wait("vector", "wl")
    e.wait("tensor", "ev")
    e.op("tensor", lambda t, nw=nw, nb=nb: t.transpose(
        c.PS[0][0:nb, 0:128], c.arena[:, nw:nw + nb], c.ident_s), "mm")
    e.wait("vector", "mm")
    e.op("vector", lambda v, nb=nb: v.tensor_copy(
        c.arena[0:nb, 47104:47232], c.PS[0][0:nb, 0:128]), "ev")
    e.op("vector", lambda v: v.tensor_copy(c.s_sq, c.s_sq), "g3")
    e.wait("gpsimd", "g3")
    e.wait("gpsimd", "ev")
    e.dma("gpsimd", lambda gp, nb=nb: gp.dma_start(
        out=d["rscr"][0:1, 0:nb * 128].rearrange("a (b c) -> (a b) c", b=nb),
        in_=c.arena[0:nb, 47104:47232]), "st")
    e.wait("sync", "st")
    e.dma("sync", lambda s, nb=nb: s.dma_start(
        out=c.arena[0:1, 43008:43008 + nb * 128], in_=d["rscr"][0:1, 0:nb * 128]), "ld")
    e.wait("tensor", "ld")
    e.wait("gpsimd", "ld")


def ph_D2(c):
    e, d = c.e, c.d
    for e8 in range(NEXP):
        e.barrier()
        e.dma("sync", lambda s, e8=e8: s.dma_start(
            out=c.arena[:, 0:8 * DFF],
            in_=d["w1e"][e8 * 128:(e8 + 1) * 128, 0:8 * DFF]), "wl")
        e.dma("sync", lambda s, e8=e8: s.dma_start(
            out=c.arena[0:1, 43008:43008 + DFF], in_=d["b1e"][e8:e8 + 1, :]), "wl")
        e.wait("tensor", "wl")
        e.wait("vector", "wl")
        for jt in range(16):
            _d2_tile1(c, TPC * e8 + 128 * jt)
    for e8 in range(NEXP):
        e.barrier()
        e.dma("sync", lambda s, e8=e8: s.dma_start(
            out=c.arena[:, 0:32 * D],
            in_=d["w2e"][e8 * 128:(e8 + 1) * 128, 0:32 * D]), "wl")
        e.dma("sync", lambda s, e8=e8: s.dma_start(
            out=c.arena[0:1, 43008:43008 + D], in_=d["b2e"][e8:e8 + 1, :]), "wl")
        e.wait("tensor", "wl")
        e.wait("vector", "wl")
        for jt in range(16):
            _d2_tile2(c, TPC * e8 + 128 * jt)


def ph_D3(c):
    e, d = c.e, c.d
    oY0, oY1, oSA, oX3 = 0, 1024, 2048, 3072
    ic0 = c.statc[:, 250:251]; ic1 = c.statc[:, 251:252]
    for j in range(NT):
        e.vop("vector", lambda v, j=j: v.tensor_copy(ic0.bitcast(I32), _rtc(c, 8, j)))
        e.vop("vector", lambda v, j=j: v.tensor_copy(ic1.bitcast(I32), _rtc(c, 9, j)))
        e.op("vector", lambda v: v.tensor_copy(c.s_sq, c.s_sq), "g3")
        e.wait("gpsimd", "g3")
        e.dma("gpsimd", lambda gp: gp.indirect_dma_start(
            A(c, oY0, D), None, d["yscr"],
            IndirectOffsetOnAxis(ap=ic0.bitcast(U32), axis=0)), "ld")
        e.dma("gpsimd", lambda gp: gp.indirect_dma_start(
            A(c, oY1, D), None, d["yscr"],
            IndirectOffsetOnAxis(ap=ic1.bitcast(U32), axis=0)), "ld")
        e.dma("sync", lambda s, j=j: s.dma_start(
            out=A(c, oSA, D), in_=d["sascr"][j * 128:(j + 1) * 128, :]), "ld")
        e.dma("sync", lambda s, j=j: s.dma_start(
            out=A(c, oX3, D), in_=d["x3scr"][j * 128:(j + 1) * 128, :]), "ld")
        lv = e.cnt["ld"]
        e.wait("vector", "ld", lv)
        e.wait("scalar", "ld", lv)
        e.vop("vector", lambda v, j=j: v.tensor_scalar(
            A(c, oY0, D), A(c, oY0, D), _rtc(c, 2, j), None, ALU.mult))
        e.vop("vector", lambda v, j=j: v.tensor_scalar(
            A(c, oY1, D), A(c, oY1, D), _rtc(c, 3, j), None, ALU.mult))
        e.vop("vector", lambda v: v.tensor_tensor(
            A(c, oSA, D), A(c, oSA, D), A(c, oY0, D), ALU.add))
        e.vop("vector", lambda v: v.tensor_tensor(
            A(c, oSA, D), A(c, oSA, D), A(c, oY1, D), ALU.add))
        e.vop("vector", lambda v: v.tensor_scalar(
            A(c, oSA, D), A(c, oSA, D), MOE_SCALE, None, ALU.mult))
        e.op("vector", lambda v: v.tensor_copy(c.s_sq, c.s_sq), "g5")
        e.wait("scalar", "g5")
        ln_tile(c, A(c, oSA, D), A(c, oSA, D), double_ln=False)
        e.op("vector", lambda v: v.tensor_tensor(
            A(c, oSA, D), A(c, oSA, D), A(c, oX3, D), ALU.add), "g2")
        e.wait("gpsimd", "g2")
        e.dma("gpsimd", lambda gp, j=j: gp.dma_start(
            out=d["y_out"][j * 128:(j + 1) * 128, :], in_=A(c, oSA, D)), "st")
        e.wait("vector", "st")
        e.wait("sync", "st")
        e.wait("gpsimd", "st")
    if DEBUG:
        _dbg_dumps(c)


def _dbg_dumps(c):
    e, d = c.e, c.d
    if DEBUG:
        e.dma("gpsimd", lambda gp: gp.dma_start(out=d["dbg_stat"], in_=c.statc[:]), "st")
        e.dma("gpsimd", lambda gp: gp.dma_start(out=d["dbg_cons"], in_=c.cons[:]), "st")
        e.dma("gpsimd", lambda gp: gp.dma_start(out=d["dbg_k0"], in_=d["kscr"][0]), "st")
        e.dma("gpsimd", lambda gp: gp.dma_start(out=d["dbg_v0"], in_=d["vscr"][0]), "st")
        e.dma("gpsimd", lambda gp: gp.dma_start(out=d["dbg_rk0"], in_=d["rkscr"][0]), "st")
        e.dma("gpsimd", lambda gp: gp.dma_start(out=d["dbg_qn"], in_=c.arena[:, 0:8192]), "st")
        e.dma("gpsimd", lambda gp: gp.dma_start(out=d["dbg_oscr"], in_=d["oscr"][0:256, :]), "st")
        e.dma("gpsimd", lambda gp: gp.dma_start(out=d["dbg_x1"], in_=d["x1scr"][0:256, :]), "st")
        e.dma("gpsimd", lambda gp: gp.dma_start(out=d["dbg_h"], in_=d["hfscr"][128:384, :]), "st")
        e.dma("gpsimd", lambda gp: gp.dma_start(out=d["dbg_xq2"], in_=d["xq2scr"][128:384, :]), "st")
        e.dma("gpsimd", lambda gp: gp.dma_start(out=d["dbg_w1q"], in_=c.arena[:, 0:8192]), "st")
        e.dma("gpsimd", lambda gp: gp.dma_start(out=d["dbg_amax"], in_=d["hfscr"][HS:HS + 32, 0:128]), "st")
        e.dma("gpsimd", lambda gp: gp.dma_start(out=d["dbg_rt"], in_=c.arena[:, oRT:oRT + 160]), "st")
        e.dma("gpsimd", lambda gp: gp.dma_start(out=d["dbg_row"], in_=c.arena[0:1, oROW:oROW + 128]), "st")
        e.dma("gpsimd", lambda gp: gp.dma_start(out=d["dbg_y"], in_=d["yscr"]), "st")
        e.dma("gpsimd", lambda gp: gp.dma_start(out=d["dbg_sa"], in_=d["sascr"]), "st")
        e.dma("gpsimd", lambda gp: gp.dma_start(out=d["dbg_x3"], in_=d["x3scr"]), "st")
        e.dma("gpsimd", lambda gp: gp.dma_start(out=d["dbg_xs"], in_=d["xsscr"]), "st")


def ph_tail_x1(c):
    e, d = c.e, c.d
    e.wait("gpsimd", "st")
    e.dma("gpsimd", lambda gp: gp.dma_start(
        out=d["y_out"], in_=d["x1scr"][1:1 + TPC, :]), "st")
    _dbg_dumps(c)





def _wide1(inputs):
    W1 = np.asarray(inputs["L1_ffn_W1"], np.float32)
    b1 = np.asarray(inputs["L1_ffn_b1"], np.float32)
    out = np.zeros((NEXP * 128, 8 * DFF + 32), np.float32)
    for e in range(NEXP):
        w1t = W1[e].T
        for ch in range(8):
            out[e * 128:(e + 1) * 128, ch * DFF:(ch + 1) * DFF] = \
                w1t[ch * 128:(ch + 1) * 128, :]
        out[e * 128:(e + 1) * 128, 8 * DFF:] = b1[e].reshape(32, 128).T
    return out


def _wide2(inputs):
    W2 = np.asarray(inputs["L1_ffn_W2"], np.float32)
    b2 = np.asarray(inputs["L1_ffn_b2"], np.float32)
    out = np.zeros((NEXP * 128, 32 * D + 8), np.float32)
    for e in range(NEXP):
        w2t = W2[e].T
        for fc in range(32):
            out[e * 128:(e + 1) * 128, fc * D:(fc + 1) * D] = \
                w2t[fc * 128:(fc + 1) * 128, :]
        out[e * 128:(e + 1) * 128, 32 * D:] = b2[e].reshape(8, 128).T
    return out


def _host_inputs(inputs, core):
    b, off = core // 4, (core % 4) * TPC
    x = np.asarray(inputs["L0_attn_x"], np.float32)
    xb = np.ascontiguousarray(x[b])
    xh = np.zeros((HS, D), np.float32)
    xh[1:1 + TPC] = xb[off:off + TPC]
    if off > 0:
        xh[0] = xb[off - 1]
    if off + TPC < SEQ:
        xh[1 + TPC] = xb[off + TPC]
    T = lambda a: np.ascontiguousarray(np.asarray(a, np.float32))

    g = np.asarray(inputs["L0_attn_ln_gamma"], np.float32)
    bb = np.asarray(inputs["L0_attn_ln_beta"], np.float32)
    convw = np.asarray(inputs["L0_ffn_conv_w"], np.float32)[:, 0, :].T

    im = {
        "x_halo": xh,
        "x_full": xb,
        "wq_t": T(np.asarray(inputs["L0_attn_w_q"]).T),
        "wk_t": T(np.asarray(inputs["L0_attn_w_k"]).T),
        "wv_t": T(np.asarray(inputs["L0_attn_w_v"]).T),
        "wo_t": T(np.asarray(inputs["L0_attn_w_o"]).T),
        "g_cols": T(g.reshape(8, 128).T),
        "b_cols": T(bb.reshape(8, 128).T),
        "qg_row": T(np.asarray(inputs["L0_attn_q_gamma"])[None, :]),
        "kg_row": T(np.asarray(inputs["L0_attn_k_gamma"])[None, :]),
        "bo_row": T(np.asarray(inputs["L0_attn_b_o"])[None, :]),
        "w1_t": T(np.asarray(inputs["L0_ffn_w1"]).T),
        "w2_t": T(np.asarray(inputs["L0_ffn_w2"]).T),
        "convw": T(convw),
        "convb_r": T(np.asarray(inputs["L0_ffn_conv_b"])[None, :]),
        "fl_lr": np.array([[0.0 if off == 0 else 1.0,
                            0.0 if off + TPC == SEQ else 1.0]], np.float32),
        "l1wq": T(inputs["L1_attn_Wq"]),
        "l1wk": T(inputs["L1_attn_Wk"]),
        "l1wv": T(inputs["L1_attn_Wv"]),
        "l1wo": T(inputs["L1_attn_Wo"]),
        "l1b4": T(np.stack([np.asarray(inputs["L1_attn_b" + k], np.float32)
                            for k in "qkvo"])),
        "wr_t": T(np.asarray(inputs["L1_ffn_w_router"]).T),
        "w1e": _wide1(inputs),
        "w2e": _wide2(inputs),
        "b1e": T(inputs["L1_ffn_b1"]),
        "b2e": T(inputs["L1_ffn_b2"]),
        "ws1_t": T(np.asarray(inputs["L1_ffn_Ws1"]).T),
        "ws2_t": T(np.asarray(inputs["L1_ffn_Ws2"]).T),
        "bs1_row": T(np.asarray(inputs["L1_ffn_bs1"])[None, :]),
        "bs2_row": T(np.asarray(inputs["L1_ffn_bs2"])[None, :]),
        "ident": np.eye(128, dtype=np.float32),
        "lstrict": np.triu(np.ones((128, 128), np.float32), 1),
        "tri8x": np.triu(np.ones((8, 8), np.float32), 1),
        "onesrow": np.ones((1, 128), np.float32),
        "tokid": np.ascontiguousarray(
            np.arange(TPC, dtype=np.float32).reshape(NT, 128).T),
        "hsum2": np.kron(np.eye(2, dtype=np.float32),
                         np.ones((64, 1), np.float32)).astype(np.float32),
        "idx8": np.tile(np.arange(NEXP, dtype=np.float32), (128, 1)),
    }
    return im


_CACHE = {}


def kernel(**inputs):
    if "nc" not in _CACHE:
        nc, stack = build_program()
        _CACHE["nc"] = nc
        _CACHE["stack"] = stack
    nc = _CACHE["nc"]
    in_maps = [_host_inputs(inputs, core) for core in range(8)]
    res = run_bass_kernel_spmd(nc, in_maps, core_ids=list(range(8)))
    if DEBUG:
        _CACHE["dbg"] = res.results
    out = np.concatenate([res.results[cc]["y_out"] for cc in range(8)], axis=0)
    return out.reshape(2, SEQ, D).astype(np.float32)
